# version 20
# baseline (speedup 1.0000x reference)
import math
from contextlib import ExitStack

import numpy as np
import ml_dtypes

import concourse.bass as bass
import concourse.mybir as mybir
from concourse.bass_utils import run_bass_kernel_spmd

F32 = mybir.dt.float32
BF16 = mybir.dt.bfloat16
AF = mybir.ActivationFunctionType
ALU = mybir.AluOpType
AX = mybir.AxisListType

D = 2048
ND = 16
FQKV = 6144
NE = 8
FF = 1024
NFC = 8
EPS_LN = 1e-5
EPS_RMS = 1e-6
ENGINES = ("tensor", "vector", "scalar", "gpsimd", "sync")


class Build:

    H_PAIRS = 2
    J_BANKS = 4

    def __init__(self, nc, NT=16, dbg=False):
        self.nc = nc
        self.dbg = dbg
        self.NT = NT
        self.T = NT * 128
        assert self.T % 512 == 0
        self.NQG = self.T // 512
        self.ops = {e: [] for e in ENGINES}
        self.sems = {}
        self.cnt = {}
        self.nbar = 0

    def emit(self, eng, fn):
        self.ops[eng].append(fn)

    def sem(self, name):
        if name not in self.sems:
            self.sems[name] = self.ctx.enter_context(self.nc.semaphore(name))
            self.cnt[name] = 0
        return self.sems[name]

    def tick(self, name, n=1):
        self.sem(name)
        self.cnt[name] += n
        return self.cnt[name]

    def dma(self, qeng, sname, out, in_, **kw):
        s = self.sem(sname)
        tgt = self.tick(sname, 16)
        self.emit(qeng, lambda e, s=s, o=out, i=in_, kw=dict(kw):
                  e.dma_start(out=o, in_=i, **kw).then_inc(s, 16))
        return tgt

    def dmat(self, sname, out, in_, qeng="sync"):
        s = self.sem(sname)
        tgt = self.tick(sname, 16)
        self.emit(qeng, lambda e, s=s, o=out, i=in_:
                  e.dma_start_transpose(o, i).then_inc(s, 16))
        return tgt

    def wait(self, eng, sname, tgt):
        if tgt <= 0:
            return
        s = self.sem(sname)
        self.emit(eng, lambda e, s=s, t=tgt: e.wait_ge(s, t))

    def mark(self, eng, semname):
        s = self.sem(semname)
        tgt = self.tick(semname)
        if eng == "tensor":
            self.emit(eng, lambda e, s=s: e.nop().then_inc(s, 1))
        else:
            self.emit(eng, lambda e, s=s: (e.drain(), e.sem_inc(s, 1)))
        return tgt

    def emit_tm(self, fn, semname="te"):
        s = self.sem(semname)
        tgt = self.tick(semname)
        self.emit("tensor", lambda e, s=s, fn=fn: fn(e).then_inc(s, 1))
        return tgt

    def barrier(self):
        self.nbar += 1
        k = self.nbar
        snap = {nm: v for nm, v in self.cnt.items() if v > 0 and not nm.startswith("bar_")}
        for e in ENGINES:
            self.sem(f"bar_{e}")
            self.cnt[f"bar_{e}"] = k
        sems = self.sems
        for e in ENGINES:
            def f(eng, e=e, snap=dict(snap), k=k):
                if e in ("vector", "scalar"):
                    eng.drain()
                for nm, v in snap.items():
                    eng.wait_ge(sems[nm], v)
                eng.sem_inc(sems[f"bar_{e}"], 1)
                for e2 in ENGINES:
                    if e2 != e:
                        eng.wait_ge(sems[f"bar_{e2}"], k)
            self.emit(e, f)

    def declare(self):
        nc, T = self.nc, self.T
        self.i_x = nc.dram_tensor("x", [T, D], F32, kind="ExternalInput").ap()
        self.i_wqkv = nc.dram_tensor("w_qkv", [FQKV, D], F32, kind="ExternalInput").ap()
        self.i_bqkv = nc.dram_tensor("b_qkv", [1, FQKV], F32, kind="ExternalInput").ap()
        self.i_wout = nc.dram_tensor("w_out", [D, D], F32, kind="ExternalInput").ap()
        self.i_bout = nc.dram_tensor("b_out", [1, D], F32, kind="ExternalInput").ap()
        self.i_rmsw = nc.dram_tensor("rms_w", [ND, 128], F32, kind="ExternalInput").ap()
        self.i_router = nc.dram_tensor("router_w", [D, NE], F32, kind="ExternalInput").ap()
        self.i_wg = nc.dram_tensor("w_gate", [NE, D, FF], F32, kind="ExternalInput").ap()
        self.i_wu = nc.dram_tensor("w_up", [NE, D, FF], F32, kind="ExternalInput").ap()
        self.i_wd = nc.dram_tensor("w_down", [NE, FF, D], F32, kind="ExternalInput").ap()
        self.i_swg = nc.dram_tensor("sw_gate", [D, FF], F32, kind="ExternalInput").ap()
        self.i_swu = nc.dram_tensor("sw_up", [D, FF], F32, kind="ExternalInput").ap()
        self.i_swd = nc.dram_tensor("sw_down", [FF, D], F32, kind="ExternalInput").ap()
        self.o_out = nc.dram_tensor("out", [T, D], F32, kind="ExternalOutput").ap()
        kd = dict(kind="ExternalOutput") if self.dbg else {}
        self.d_xn = nc.dram_tensor("xn_d", [T, D], BF16, **kd).ap()
        self.d_wqkv_bf = nc.dram_tensor("wqkv_bf", [FQKV, D], BF16, **kd).ap()
        self.d_wqkvT = nc.dram_tensor("wqkvT", [D, FQKV], BF16, **kd).ap()
        self.d_wout_bf = nc.dram_tensor("wout_bf", [D, D], BF16).ap()
        self.d_woutT = nc.dram_tensor("woutT", [D, D], BF16).ap()
        self.d_q = nc.dram_tensor("q_d", [D, T], BF16, **kd).ap()
        self.d_v = nc.dram_tensor("v_d", [T, D], BF16, **kd).ap()
        self.d_probs = nc.dram_tensor("probs_d", [T, T], BF16, **kd).ap()
        self.d_ctx = nc.dram_tensor("ctx_d", [D, T], BF16, **kd).ap()
        self.d_moe = nc.dram_tensor("moe_d", [T, D], F32, **kd).ap()
        self.c_ones = nc.inline_tensor(np.ones((1, 512), dtype=ml_dtypes.bfloat16), "c_ones").ap()
        self.c_ident = nc.inline_tensor(np.eye(128, dtype=np.float32), "c_ident").ap()
        self.c_identb = nc.inline_tensor(np.eye(128, dtype=ml_dtypes.bfloat16), "c_identb").ap()
        basis = np.zeros((8, NE * 128), dtype=np.float32)
        for e in range(NE):
            basis[e, e * 128:(e + 1) * 128] = 1.0
        self.c_basis = nc.inline_tensor(basis, "c_basis").ap()

    def alloc_persist(self, ctx):
        nc, T = self.nc, self.T
        sb = lambda n, sh, dt: ctx.enter_context(nc.sbuf_tensor(n, sh, dt))
        self.ones_s = sb("ones_s", [1, 512], BF16)
        self.ident_s = sb("ident_s", [128, 128], F32)
        self.identb_s = sb("identb_s", [128, 128], BF16)
        self.basis_s = sb("basis_s", [8, NE * 128], F32)
        self.wfm = sb("wfm", [128, ND], F32)
        self.psA = ctx.enter_context(nc.psum_tensor("psA", [128, 2048], F32))
        self.psB = ctx.enter_context(nc.psum_tensor("psB", [128, 2048], F32))
        self.dma("gpsimd", "g_const", self.ones_s[:, :], self.c_ones)
        self.dma("gpsimd", "g_const", self.ident_s[:, :], self.c_ident)
        self.dma("gpsimd", "g_const", self.identb_s[:, :], self.c_identb)
        self.dma("gpsimd", "g_const", self.basis_s[:, :], self.c_basis)
        self.const_tgt = self.cnt["g_const"]

    def bank(self, which, i):
        t = self.psA if which == "A" else self.psB
        return t[:, i * 512:(i + 1) * 512]

    def p0_ln(self, ctx):
        nc, NT, T = self.nc, self.NT, self.T
        sb = lambda n, sh, dt: ctx.enter_context(nc.sbuf_tensor(n, sh, dt))
        W = 5
        xch = [sb(f"xch{i}", [128, D], F32) for i in range(W)]
        xnch = [sb(f"xnch{i}", [128, D], BF16) for i in range(W)]
        xn2ch = [sb(f"xn2ch{i}", [128, D], BF16) for i in range(W)]
        sq = [sb(f"sq{i}", [128, D], BF16) for i in range(W)]
        s2 = [sb(f"s2_{i}", [128, 1], F32) for i in range(W)]
        mean = [sb(f"mean{i}", [128, 1], F32) for i in range(W)]
        var = [sb(f"var{i}", [128, 1], F32) for i in range(W)]
        rstd = [sb(f"rstd{i}", [128, 1], F32) for i in range(W)]
        rs2 = [sb(f"rs2_{i}", [128, 2], F32) for i in range(W)]
        rsum = [sb(f"rsum{i}", [128, 1], F32) for i in range(W)]
        w16 = sb("w16", [16, 128], F32)
        wt = self.dma("sync", "s_misc", w16[:, :], self.i_rmsw)
        self.wait("tensor", "s_misc", wt)
        self.wait("tensor", "g_const", self.const_tgt)
        tt = self.emit_tm(lambda e: e.transpose(self.psB[0:128, 0:16], w16[:, :],
                                                self.ident_s[0:16, 0:16]))
        self.wait("vector", "te", tt)
        self.emit("vector", lambda e: e.tensor_copy(self.wfm[:, :], self.psB[0:128, 0:16]))
        self.wfm_ready = self.mark("vector", "ve")
        done_v = [0] * W
        done_a = [0] * W
        done_x = [0] * W
        tr_done = {}
        ev_done = {}
        xt = {}
        at_sq = {}
        vt_var = {}
        at_rs = {}
        vt_xn = {}
        at_sq2 = {}
        vt_rv = {}
        at_s2 = {}
        for w0 in range(0, NT, W):
            wave = range(w0, min(w0 + W, NT))
            for tc in wave:
                b = tc % W
                if done_v[b]:
                    self.wait("sync", "ve", done_v[b])
                    self.wait("sync", "ae", done_a[b])
                    self.wait("sync", "te", tr_done[tc - W])
                xt[tc] = self.dma("sync", f"s_x{b}", xch[b][:, :],
                                  self.i_x[tc * 128:(tc + 1) * 128, :])
            for tc in wave:
                b = tc % W
                self.wait("scalar", f"s_x{b}", xt[tc])
                self.emit("scalar", lambda e: e.drain())
                self.emit("scalar", lambda e, b=b: e.activation(
                    sq[b][:, :], xch[b][:, :], AF.Square, accum_out=s2[b][:, :]))
                at_sq[tc] = self.mark("scalar", "ae")
            for tc in wave:
                b = tc % W
                self.wait("vector", f"s_x{b}", xt[tc])
                self.emit("vector", lambda e, b=b: (
                    e.reduce_sum(mean[b][:, :], xch[b][:, :], axis=AX.X), e.drain(),
                    e.tensor_scalar(mean[b][:, :], mean[b][:, :], 1.0 / D, None, op0=ALU.mult)))
                self.mark("vector", "ve")
            for tc in wave:
                b = tc % W
                self.wait("vector", "ae", at_sq[tc])
                self.emit("vector", lambda e, b=b: (
                    e.tensor_mul(var[b][:, :], mean[b][:, :], mean[b][:, :]), e.drain(),
                    e.tensor_scalar(s2[b][:, :], s2[b][:, :], 1.0 / D, EPS_LN,
                                    op0=ALU.mult, op1=ALU.add), e.drain(),
                    e.tensor_sub(rs2[b][:, 0:1], s2[b][:, :], var[b][:, :]), e.drain(),
                    e.reciprocal(rsum[b][:, :], rs2[b][:, 0:1]), e.drain(),
                    e.tensor_scalar(var[b][:, :], rs2[b][:, 0:1], -EPS_LN, None, op0=ALU.add),
                    e.drain(),
                    e.tensor_mul(var[b][:, :], var[b][:, :], rsum[b][:, :]), e.drain(),
                    e.tensor_scalar(rs2[b][:, 1:2], var[b][:, :], 1.0, EPS_RMS,
                                    op0=ALU.mult, op1=ALU.add)))
                vt_var[tc] = self.mark("vector", "ve")
            for tc in wave:
                b = tc % W
                self.wait("scalar", "ve", vt_var[tc])
                self.emit("scalar", lambda e: e.drain())
                self.emit("scalar", lambda e, b=b: e.activation(rs2[b][:, :], rs2[b][:, :], AF.Sqrt))
                at_rs[tc] = self.mark("scalar", "ae")
            for tc in wave:
                b = tc % W
                self.wait("vector", "ae", at_rs[tc])
                if done_x[b]:
                    self.wait("vector", f"s_xn{b}", done_x[b])
                if tc >= W:
                    self.wait("vector", "te", tr_done[tc - W])
                self.emit("vector", lambda e, b=b: (
                    e.reciprocal(rstd[b][:, :], rs2[b][:, 0:1]), e.drain(),
                    e.reciprocal(rsum[b][:, :], rs2[b][:, 1:2]), e.drain(),
                    e.tensor_mul(rsum[b][:, :], rsum[b][:, :], rstd[b][:, :]), e.drain(),
                    e.tensor_scalar(xnch[b][:, :], xch[b][:, :], mean[b][:, :], rstd[b][:, :],
                                    op0=ALU.subtract, op1=ALU.mult), e.drain(),
                    e.tensor_scalar(xn2ch[b][:, :], xch[b][:, :], mean[b][:, :], rsum[b][:, :],
                                    op0=ALU.subtract, op1=ALU.mult)))
                done_v[b] = self.mark("vector", "ve")
                done_a[b] = at_sq[tc]
                self.wait("sync", "ve", done_v[b])
                self.dma("sync", f"s_xn{b}", self.d_xn[tc * 128:(tc + 1) * 128, :], xnch[b][:, :])
                done_x[b] = self.cnt[f"s_xn{b}"]
            for tc in wave:
                b = tc % W
                self.wait("tensor", "ve", done_v[b])
                for half in range(2):
                    bank = (tc % 2) * 2 + half
                    pv = self.psA[:, bank * 512:(bank + 1) * 512].bitcast(BF16)
                    self.wait("tensor", "ve", ev_done.get((tc - 2, half), 0))
                    def trg(e, b=b, half=half, pv=pv):
                        for j in range(8):
                            dc = half * 8 + j
                            last = e.transpose(pv[:, j * 128:(j + 1) * 128],
                                               xn2ch[b][:, dc * 128:(dc + 1) * 128],
                                               self.identb_s[:, :])
                        return last
                    ttk = self.emit_tm(trg)
                    tr_done[tc] = ttk
                    self.wait("vector", "te", ttk)
                    def evac(e, tc=tc, half=half, pv=pv):
                        e.tensor_copy(
                            self.ht[:, 0:ND * T]
                            .rearrange("p (dc t) -> p dc t", dc=ND)[:, half * 8:(half + 1) * 8,
                                                                   tc * 128:(tc + 1) * 128],
                            pv[:, :].rearrange("p (dc t) -> p dc t", dc=8))
                    self.emit("vector", evac)
                    ev_done[(tc, half)] = self.mark("vector", "ve")
        self.ln_ev_done = ev_done
        self.ln_tr_done = tr_done

    def p1_ht(self):
        T = self.T
        self.wait("vector", "ve", self.wfm_ready)
        self.wait("vector", "ve", self.ln_ev_done[(self.NT - 1, 1)])
        for dc in range(ND):
            self.emit("vector", lambda e, dc=dc: e.tensor_scalar(
                self.ht[:, dc * T:(dc + 1) * T], self.ht[:, dc * T:(dc + 1) * T],
                self.wfm[:, dc:dc + 1], None, op0=ALU.mult))
            self.mark("vector", "ve")
        self.ht_ready = self.cnt["ve"]

    def p2_router(self, ctx):
        nc, NT, T = self.nc, self.NT, self.T
        sb = lambda n, sh, dt: ctx.enter_context(nc.sbuf_tensor(n, sh, dt))
        wr = sb("wr", [128, ND * NE], BF16)
        probs8 = sb("probs8", [128, NE], F32)
        maxs8 = sb("maxs8", [128, NE], F32)
        sel8 = sb("sel8", [128, NE], F32)
        wd_all = sb("wd_all", [128, NE], F32)
        r1 = sb("r1", [128, 1], F32)
        r2 = sb("r2", [128, 1], F32)
        r3 = sb("r3", [128, 1], F32)
        wrt = self.dma("gpsimd", "g_wr", wr[:, :].rearrange("p (dc e) -> p dc e", dc=ND),
                       self.i_router.rearrange("(dc p) e -> p dc e", p=128))
        self.wait("tensor", "g_wr", wrt)
        self.wait("tensor", "ve", self.ht_ready)
        rt_done = {}
        for tc in range(NT):
            pb = self.bank("A", tc % 2)
            if tc >= 2:
                self.wait("tensor", "ve", rt_done[tc - 2])
            def mm_router(e, tc=tc, pb=pb):
                for dc in range(ND):
                    last = e.matmul(pb[:, 0:NE],
                                    self.ht[:, dc * T + tc * 128: dc * T + (tc + 1) * 128],
                                    wr[:, dc * NE:(dc + 1) * NE],
                                    start=(dc == 0), stop=(dc == ND - 1))
                return last
            tt = self.emit_tm(mm_router)
            self.wait("vector", "te", tt)
            self.emit("vector", lambda e, pb=pb: (
                e.reduce_max(r1[:, :], pb[:, 0:NE], axis=AX.X), e.drain(),
                e.tensor_scalar(r1[:, :], r1[:, :], -1.0, None, op0=ALU.mult)))
            vt = self.mark("vector", "ve")
            self.wait("scalar", "ve", vt)
            self.emit("scalar", lambda e: e.drain())
            self.emit("scalar", lambda e, pb=pb: e.activation(
                probs8[:, :], pb[:, 0:NE], AF.Exp, bias=r1[:, :], accum_out=r2[:, :]))
            at = self.mark("scalar", "ae")
            self.wait("vector", "ae", at)
            self.emit("vector", lambda e: (
                e.reciprocal(r3[:, :], r2[:, :]), e.drain(),
                e.tensor_scalar(probs8[:, :], probs8[:, :], r3[:, :], None, op0=ALU.mult), e.drain(),
                e.max(out=maxs8[:, :], in_=probs8[:, :]), e.drain(),
                e.tensor_scalar(sel8[:, :], probs8[:, :], maxs8[:, 1:2], None, op0=ALU.is_ge),
                e.drain(),
                e.tensor_add(r1[:, :], maxs8[:, 0:1], maxs8[:, 1:2]), e.drain(),
                e.reciprocal(r2[:, :], r1[:, :]), e.drain(),
                e.scalar_tensor_tensor(wd_all[:, :], probs8[:, :], r2[:, :], sel8[:, :],
                                       op0=ALU.mult, op1=ALU.mult)))
            vt2 = self.mark("vector", "ve")
            self.wait("tensor", "ve", vt2)
            tt2 = self.emit_tm(lambda e, tc=tc: e.transpose(
                self.psB[0:8, (tc % 2) * 512:(tc % 2) * 512 + 128], wd_all[:, :],
                self.ident_s[:, :]))
            self.wait("vector", "te", tt2)
            self.emit("vector", lambda e, tc=tc: e.tensor_copy(
                self.wdT[0:8, tc * 128:(tc + 1) * 128],
                self.psB[0:8, (tc % 2) * 512:(tc % 2) * 512 + 128]))
            rt_done[tc] = self.mark("vector", "ve")
        self.wd_ready = self.cnt["ve"]

    def p3_moe(self, ctx):
        nc, NT, T, NQG = self.nc, self.NT, self.T, self.NQG
        sb = lambda n, sh, dt: ctx.enter_context(nc.sbuf_tensor(n, sh, dt))
        wgb = [sb(f"wgb{i}", [128, ND * 128], BF16) for i in range(2)]
        wub = [sb(f"wub{i}", [128, ND * 128], BF16) for i in range(2)]
        wdb = [sb(f"wdb{i}", [128, NFC * 1024], BF16) for i in range(2)]
        a_s = sb("a_s", [128, NFC * T], BF16)
        silu_b = [sb(f"silu{i}", [128, 512], BF16) for i in range(2)]
        tmul_b = [sb(f"tmul{i}", [128, 512], BF16) for i in range(2)]
        jbuf = [sb(f"jbuf{i}", [128, 2048], F32) for i in range(3)]

        wg_t = {}; wu_t = {}; wd_t = {}
        def prefetch_gu(e, fc, parity, after_te=0):
            self.wait("gpsimd", "te", after_te)
            src_g = self.i_wg[e, :, :] if e < NE else self.i_swg
            src_u = self.i_wu[e, :, :] if e < NE else self.i_swu
            wg_t[(e, fc)] = self.dma(
                "gpsimd", f"g_wg{parity}",
                wgb[parity][:, :].rearrange("p (dc f) -> p dc f", dc=ND),
                src_g[:, fc * 128:(fc + 1) * 128].rearrange("(dc p) f -> p dc f", p=128))
            wu_t[(e, fc)] = self.dma(
                "gpsimd", f"g_wu{parity}",
                wub[parity][:, :].rearrange("p (dc f) -> p dc f", dc=ND),
                src_u[:, fc * 128:(fc + 1) * 128].rearrange("(dc p) f -> p dc f", p=128))

        def prefetch_wd(e, half, parity, after_te=0):
            self.wait("gpsimd", "te", after_te)
            src = self.i_wd[e, :, :] if e < NE else self.i_swd
            wd_t[(e, half)] = self.dma(
                "gpsimd", f"g_wd{parity}",
                wdb[parity][:, :].rearrange("p (fc d) -> p fc d", fc=NFC),
                src[:, half * 1024:(half + 1) * 1024].rearrange("(fc p) d -> p fc d", p=128))

        prefetch_gu(0, 0, 0)
        prefetch_gu(0, 1, 1)
        prefetch_wd(0, 0, 0)
        slabs = self.staging_cast_slabs()
        def bg_slab(n=1):
            for _ in range(n):
                if slabs:
                    slabs.pop(0)()

        self.wait("tensor", "ve", self.wd_ready)
        gu_mm_done = {}
        wd_mm_done = {}
        wb_free = {}
        j_free = {}
        h_free = {}
        jb_dma = {}
        wb_ready_t = {}
        wb_buf = [self.wb[:, 0:T], self.wb2[:, 0:T]]

        def wb_build(e, h_free=None):
            dst = wb_buf[e % 2]
            ttks = []
            for tt in range(NQG):
                pb = self.bank("A", tt % 4)
                self.wait("tensor", "ve", wb_free.get(tt % 4, 0))
                if h_free:
                    self.wait("tensor", "ve", h_free.get((tt % 4) // 2, 0))
                    self.wait("tensor", "ve", h_free.get(2, 0))
                ttks.append(self.emit_tm(lambda en, e=e, tt=tt, pb=pb: en.matmul(
                    pb[:, :], self.basis_s[:, e * 128:(e + 1) * 128],
                    self.wdT[0:8, tt * 512:(tt + 1) * 512], start=True, stop=True)))
            for tt in range(NQG):
                pb = self.bank("A", tt % 4)
                self.wait("vector", "te", ttks[tt])
                self.emit("vector", lambda en, tt=tt, pb=pb, dst=dst: en.tensor_copy(
                    dst[:, tt * 512:(tt + 1) * 512], pb[:, :]))
                wb_free[tt % 4] = self.mark("vector", "ve")
            wb_ready_t[e] = self.cnt["ve"]

        wb_build(0)
        for e in range(NE + 1):
            wb_ready = wb_ready_t.get(e, 0)

            a_done_v = 0
            pairs = [("A", 0), ("A", 2), ("B", 0)][:self.H_PAIRS]
            for fc in range(NFC):
                par = fc % 2
                for tt in range(NQG):
                    slot = (fc * NQG + tt) % self.H_PAIRS
                    w, b0 = pairs[slot]
                    pg = self.bank(w, b0)
                    pu = self.bank(w, b0 + 1)
                    self.wait("tensor", f"g_wg{par}", wg_t[(e, fc)])
                    self.wait("tensor", f"g_wu{par}", wu_t[(e, fc)])
                    if wb_ready and fc == 0 and tt == 0:
                        self.wait("tensor", "ve", wb_ready)
                    self.wait("tensor", "ve", h_free.get(slot, 0))
                    if w == "B":
                        self.wait("tensor", "ve", j_free.get(b0, 0))
                        self.wait("tensor", "ve", j_free.get(b0 + 1, 0))
                    def mm_g(en, tt=tt, par=par, pg=pg):
                        for dc in range(ND):
                            last = en.matmul(pg[:, :],
                                             wgb[par][:, dc * 128:(dc + 1) * 128],
                                             self.ht[:, dc * T + tt * 512: dc * T + (tt + 1) * 512],
                                             start=(dc == 0), stop=(dc == ND - 1))
                        return last
                    tg = self.emit_tm(mm_g)
                    def mm_u(en, tt=tt, par=par, pu=pu):
                        for dc in range(ND):
                            last = en.matmul(pu[:, :],
                                             wub[par][:, dc * 128:(dc + 1) * 128],
                                             self.ht[:, dc * T + tt * 512: dc * T + (tt + 1) * 512],
                                             start=(dc == 0), stop=(dc == ND - 1))
                        return last
                    tu = self.emit_tm(mm_u)
                    gu_mm_done[(e, fc)] = tu
                    self.wait("scalar", "te", tg)
                    self.wait("scalar", "ve", h_free.get(slot, 0))
                    self.emit("scalar", lambda en: en.drain())
                    self.emit("scalar", lambda en, slot=slot, pg=pg: en.activation(
                        silu_b[slot][:, :], pg[:, :], AF.Silu))
                    atk = self.mark("scalar", "ae")
                    self.wait("vector", "te", tu)
                    if e < NE:
                        self.emit("vector", lambda en, e=e, tt=tt, slot=slot, pu=pu:
                                  en.tensor_mul(tmul_b[slot][:, :], pu[:, :],
                                                wb_buf[e % 2][:, tt * 512:(tt + 1) * 512]))
                        self.emit("vector", lambda en: en.drain())
                        self.wait("vector", "ae", atk)
                        self.emit("vector", lambda en, fc=fc, tt=tt, slot=slot:
                                  en.tensor_mul(a_s[:, fc * T + tt * 512: fc * T + (tt + 1) * 512],
                                                silu_b[slot][:, :], tmul_b[slot][:, :]))
                    else:
                        self.wait("vector", "ae", atk)
                        self.emit("vector", lambda en, fc=fc, tt=tt, slot=slot, pu=pu:
                                  en.tensor_mul(a_s[:, fc * T + tt * 512: fc * T + (tt + 1) * 512],
                                                silu_b[slot][:, :], pu[:, :]))
                    vtk = self.mark("vector", "ve")
                    h_free[slot] = vtk
                    a_done_v = vtk
                if fc + 2 < NFC:
                    prefetch_gu(e, fc + 2, par, after_te=gu_mm_done[(e, fc)])
                elif e < NE:
                    prefetch_gu(e + 1, fc + 2 - NFC, par, after_te=gu_mm_done[(e, fc)])
                bg_slab(1 if e == 0 else 1)

            for half in range(2):
                parj = half
                if half == 0:
                    prefetch_wd(e, 1, 1, after_te=wd_mm_done.get((e - 1, 1), 0))
                elif e < NE:
                    prefetch_wd(e + 1, 0, 0, after_te=wd_mm_done.get((e, 0), 0))
                if half == 1 and e + 1 < NE:
                    wb_build(e + 1, h_free=h_free)
                self.wait("tensor", f"g_wd{parj}", wd_t[(e, half)])
                self.wait("tensor", "ve", a_done_v)
                for tc in range(NT):
                    slotj = (tc // 2) % 3
                    sub = tc % 2
                    for dt in range(2):
                        jb = (tc * 2 + dt) % 4 if self.J_BANKS == 4 else 2 + dt
                        pj = self.bank("B", jb)
                        self.wait("tensor", "ve", j_free.get(jb, 0))
                        if jb < 2:
                            self.wait("tensor", "ve", h_free.get(2, 0))
                        def mm_j(en, tc=tc, dt=dt, parj=parj, pj=pj):
                            for fc in range(NFC):
                                last = en.matmul(pj[:, :],
                                                 a_s[:, fc * T + tc * 128: fc * T + (tc + 1) * 128],
                                                 wdb[parj][:, fc * 1024 + dt * 512: fc * 1024 + (dt + 1) * 512],
                                                 start=(fc == 0), stop=(fc == NFC - 1))
                            return last
                        ttk = self.emit_tm(mm_j)
                        wd_mm_done[(e, half)] = ttk
                        self.wait("vector", "te", ttk)
                        self.wait("vector", f"g_acc{slotj}", jb_dma.get(slotj, 0))
                        self.emit("vector", lambda en, slotj=slotj, sub=sub, dt=dt, pj=pj:
                                  en.tensor_copy(jbuf[slotj][:, sub * 1024 + dt * 512:
                                                             sub * 1024 + (dt + 1) * 512], pj[:, :]))
                        j_free[jb] = self.mark("vector", "ve")
                    if sub == 1:
                        self.wait("gpsimd", "ve", self.cnt["ve"])
                        kw = {} if e == 0 else {"accum_op": ALU.add}
                        jb_dma[slotj] = self.dma(
                            "gpsimd", f"g_acc{slotj}",
                            self.d_moe[(tc - 1) * 128:(tc + 1) * 128,
                                       half * 1024:(half + 1) * 1024]
                            .rearrange("(two p) d -> p two d", p=128),
                            jbuf[slotj][:, :].rearrange("p (two d) -> p two d", two=2), **kw)
        bg_slab(len(slabs))
        self.moe_last_h = gu_mm_done[(NE, NFC - 1)]
        self.moe_acc_t = {k: self.cnt.get(k, 0) for k in ("g_acc0", "g_acc1")}

    def staging_cast_slabs(self):
        slabs = []
        for r in range(0, FQKV, 512):
            slabs.append(lambda r=r: self.dma("gpsimd", "g_stgc",
                                              self.d_wqkv_bf[r:r + 512, :],
                                              self.i_wqkv[r:r + 512, :]))
        for r in range(0, D, 512):
            slabs.append(lambda r=r: self.dma("gpsimd", "g_stgc",
                                              self.d_wout_bf[r:r + 512, :],
                                              self.i_wout[r:r + 512, :]))
        return slabs

    def stage_weights(self, ctx):
        nc = self.nc
        HB = FQKV // 3
        bounce = ctx.enter_context(nc.sbuf_tensor("stg", [128, HB], BF16))
        cast_done = self.cnt.get("g_stgc", 0)
        self.wait("sync", "g_stgc", cast_done)
        prev = 0
        for dc in range(ND):
            for fh in range(3):
                self.wait("sync", "s_stgw", prev)
                t = self.dmat("s_stgt", bounce[:, :],
                              self.d_wqkv_bf[fh * HB:(fh + 1) * HB, dc * 128:(dc + 1) * 128])
                self.wait("sync", "s_stgt", t)
                prev = self.dma("sync", "s_stgw",
                                self.d_wqkvT[dc * 128:(dc + 1) * 128, fh * HB:(fh + 1) * HB],
                                bounce[:, :])
        for dc in range(ND):
            self.wait("sync", "s_stgw", prev)
            t = self.dmat("s_stgt", bounce[:, 0:D], self.d_wout_bf[:, dc * 128:(dc + 1) * 128])
            self.wait("sync", "s_stgt", t)
            prev = self.dma("sync", "s_stgw", self.d_woutT[dc * 128:(dc + 1) * 128, :],
                            bounce[:, 0:D])
        self.stage_done = self.cnt["s_stgw"]

    def p4_qkv(self, ctx):
        nc, NT, T, NQG = self.nc, self.NT, self.T, self.NQG
        sb = lambda n, sh, dt: ctx.enter_context(nc.sbuf_tensor(n, sh, dt))
        wqp = [sb(f"wqp{i}", [128, ND * 128], BF16) for i in range(2)]
        vp = [sb(f"vp{i}", [128, ND * 512], BF16) for i in range(2)]
        evb = [sb(f"evb{i}", [128, 512], BF16) for i in range(4)]
        bqh = sb("bqh", [1, FQKV], BF16)
        bt = self.dma("gpsimd", "g_misc", bqh[:, :], self.i_bqkv)
        self.wait("tensor", "g_misc", bt)
        self.wait("tensor", "xn_sync", self.xn_t)
        piece_t = {}
        par_done = {}
        qkv_free = {}
        ev_dma = {}
        for i, fc in enumerate(range(0, 32)):
            par = i % 2
            self.wait("sync", "te", par_done.get(par, 0))
            piece_t[fc] = self.dma("sync", f"s_wqp{par}",
                                   wqp[par][:, :].rearrange("p (dc f) -> p dc f", dc=ND),
                                   self.d_wqkvT[:, fc * 128:(fc + 1) * 128]
                                   .rearrange("(dc p) f -> p dc f", p=128))
            self.wait("tensor", f"s_wqp{par}", piece_t[fc])
            for tt in range(NQG):
                slot = (i * NQG + tt) % 4
                pq = self.bank("A", slot)
                self.wait("tensor", "ve", qkv_free.get(slot, 0))
                def mm_qk(en, fc=fc, tt=tt, par=par, pq=pq):
                    for dc in range(ND):
                        en.matmul(pq[:, :], wqp[par][:, dc * 128:(dc + 1) * 128],
                                  self.ht[:, dc * T + tt * 512: dc * T + (tt + 1) * 512],
                                  start=(dc == 0), stop=False)
                    return en.matmul(pq[:, :], bqh[0:1, fc * 128:(fc + 1) * 128],
                                     self.ones_s[0:1, :], start=False, stop=True)
                ttk = self.emit_tm(mm_qk)
                par_done[par] = ttk
                self.wait("vector", "te", ttk)
                if fc < 16:
                    eb = evb[slot]
                    self.wait("vector", f"c_ev{slot}", ev_dma.get(slot, 0))
                    self.emit("vector", lambda en, eb=eb, pq=pq: en.tensor_copy(eb[:, :], pq[:, :]))
                    vtk = self.mark("vector", "ve")
                    self.wait("scalar", "ve", vtk)
                    ev_dma[slot] = self.dma(
                        "scalar", f"c_ev{slot}",
                        self.d_q[fc * 128:(fc + 1) * 128, tt * 512:(tt + 1) * 512], eb[:, :])
                else:
                    hc = fc - 16
                    self.emit("vector", lambda en, hc=hc, tt=tt, pq=pq: en.tensor_copy(
                        self.big[:, hc * T + tt * 512: hc * T + (tt + 1) * 512], pq[:, :]))
                    vtk = self.mark("vector", "ve")
                qkv_free[slot] = vtk
        vp_used = {}
        for hti in range(4):
            vpar = hti % 2
            self.wait("sync", "te", vp_used.get(vpar, 0))
            vt = self.dma("sync", f"s_vp{vpar}",
                          vp[vpar][:, :].rearrange("p (dc f) -> p dc f", dc=ND),
                          self.d_wqkvT[:, 4096 + hti * 512: 4096 + (hti + 1) * 512]
                          .rearrange("(dc p) f -> p dc f", p=128))
            self.wait("tensor", f"s_vp{vpar}", vt)
            for tc in range(NT):
                slot = (hti * NT + tc) % 4
                pv = self.bank("A", slot)
                self.wait("tensor", "ve", qkv_free.get(slot, 0))
                def mm_v(en, hti=hti, tc=tc, pv=pv, vpar=vpar):
                    for dc in range(ND):
                        en.matmul(pv[:, :], self.ht[:, dc * T + tc * 128: dc * T + (tc + 1) * 128],
                                  vp[vpar][:, dc * 512:(dc + 1) * 512],
                                  start=(dc == 0), stop=False)
                    return en.matmul(pv[:, :], self.ones_s[0:1, 0:128],
                                     bqh[0:1, 4096 + hti * 512: 4096 + (hti + 1) * 512],
                                     start=False, stop=True)
                ttk = self.emit_tm(mm_v)
                vp_used[vpar] = ttk
                self.wait("vector", "te", ttk)
                eb = evb[slot]
                self.wait("vector", f"c_ev{slot}", ev_dma.get(slot, 0))
                self.emit("vector", lambda en, eb=eb, pv=pv: en.tensor_copy(eb[:, :], pv[:, :]))
                vtk = self.mark("vector", "ve")
                self.wait("scalar", "ve", vtk)
                ev_dma[slot] = self.dma(
                    "scalar", f"c_ev{slot}",
                    self.d_v[tc * 128:(tc + 1) * 128, hti * 512:(hti + 1) * 512], eb[:, :])
                qkv_free[slot] = vtk

    def p5_scores(self, ctx):
        nc, NT, T, NQG = self.nc, self.NT, self.T, self.NQG
        sb = lambda n, sh, dt: ctx.enter_context(nc.sbuf_tensor(n, sh, dt))
        qg = [sb(f"qg{i}", [128, ND * 512], BF16) for i in range(2)]
        prb = [sb(f"prb{i}", [128, T], BF16) for i in range(2)]
        rm = sb("rm", [128, 1], F32)
        rm4 = sb("rm4", [128, 16], F32)
        se = sb("se", [128, 1], F32)
        ri = sb("ri", [128, 1], F32)
        SCALE = 1.0 / math.sqrt(float(D))
        self.pT_pre = {}
        qg_used = {}
        exp_done = {}
        pr_dma = {}
        q_t = {}
        def load_q(g):
            par = g % 2
            self.wait("gpsimd", "te", qg_used.get(par, 0))
            q_t[g] = self.dma("gpsimd", f"g_qg{par}",
                              qg[par][:, :].rearrange("p (hc q) -> p hc q", hc=ND),
                              self.d_q[:, g * 512:(g + 1) * 512]
                              .rearrange("(hc p) q -> p hc q", p=128))
        load_q(0)
        for g in range(NQG):
            par = g % 2
            if g + 1 < NQG:
                load_q(g + 1)
            for kc in range(g * NT // NQG, (g + 1) * NT // NQG):
                self.dma("gpsimd", "g_vload", self.vs_s[:, kc * D:(kc + 1) * D],
                         self.d_v[kc * 128:(kc + 1) * 128, :])
            self.wait("tensor", f"g_qg{par}", q_t[g])
            for qc in range(4):
                i = g * 4 + qc
                ps_full = self.psA if i % 2 == 0 else self.psB
                self.wait("tensor", "ae", exp_done.get(i - 2, 0))
                kt_marks = []
                for kt in range(NQG):
                    def mm_kt(en, par=par, qc=qc, ps_full=ps_full, kt=kt):
                        for hc in range(ND):
                            last = en.matmul(ps_full[:, kt * 512:(kt + 1) * 512],
                                             qg[par][:, hc * 512 + qc * 128: hc * 512 + (qc + 1) * 128],
                                             self.big[:, hc * T + kt * 512: hc * T + (kt + 1) * 512],
                                             start=(hc == 0), stop=(hc == ND - 1))
                        return last
                    kt_marks.append(self.emit_tm(mm_kt))
                ttk = kt_marks[-1]
                qg_used[par] = ttk
                for kt in range(NQG):
                    self.wait("vector", "te", kt_marks[kt])
                    self.emit("vector", lambda en, ps_full=ps_full, kt=kt: (
                        en.reduce_max(rm4[:, kt:kt + 1], ps_full[:, kt * 512:(kt + 1) * 512],
                                      axis=AX.X), en.drain()))
                self.emit("vector", lambda en: (
                    en.reduce_max(rm[:, :], rm4[:, 0:NQG], axis=AX.X), en.drain(),
                    en.tensor_scalar(rm[:, :], rm[:, :], -SCALE, None, op0=ALU.mult)))
                vtk = self.mark("vector", "ve")
                bpr = i % 2
                self.wait("scalar", f"s_pr{bpr}", pr_dma.get(bpr, 0))
                self.wait("scalar", "ve", vtk)
                self.emit("scalar", lambda en: en.drain())
                self.emit("scalar", lambda en, ps_full=ps_full, bpr=bpr: en.activation(
                    prb[bpr][:, :], ps_full[:, 0:T], AF.Exp, bias=rm[:, :], scale=SCALE,
                    accum_out=se[:, :]))
                atk = self.mark("scalar", "ae")
                exp_done[i] = atk
                self.wait("vector", "ae", atk)
                self.emit("vector", lambda en, bpr=bpr: (
                    en.reciprocal(ri[:, :], se[:, :]), en.drain(),
                    en.tensor_scalar(prb[bpr][:, :], prb[bpr][:, :], ri[:, :], None, op0=ALU.mult)))
                vtk2 = self.mark("vector", "ve")
                self.wait("sync", "ve", vtk2)
                row = g * 512 + qc * 128
                pr_dma[bpr] = self.dma("sync", f"s_pr{bpr}",
                                       self.d_probs[row:row + 128, :], prb[bpr][:, :])
            if g < 2:
                self.wait("sync", "s_pr0", self.cnt.get("s_pr0", 0))
                self.wait("sync", "s_pr1", self.cnt.get("s_pr1", 0))
                par = g % 2
                t1 = 0
                for kc in range(NT):
                    t1 = self.dmat(f"pT{par}_s", self.pT[par][:, kc * 512:(kc + 1) * 512],
                                   self.d_probs[g * 512:(g + 1) * 512, kc * 128:(kc + 1) * 128])
                self.pT_pre[g] = t1
        self.v_t = self.cnt["g_vload"]

    def p6_ctx(self, ctx):
        nc, NT, T, NQG = self.nc, self.NT, self.T, self.NQG
        sb = lambda n, sh, dt: ctx.enter_context(nc.sbuf_tensor(n, sh, dt))
        pT = self.pT
        cb = [sb(f"cb{i}", [128, 512], BF16) for i in range(4)]
        self.wait("tensor", "g_vload", self.v_t)
        pT_used = {}
        cb_dma = {}
        ctx_free = {}
        for g in range(NQG):
            par = g % 2
            if g in self.pT_pre:
                t1 = self.pT_pre[g]
            else:
                self.wait("sync", "te", pT_used.get(par, 0))
                t1 = 0
                for kc in range(NT):
                    t1 = self.dmat(f"pT{par}_s", pT[par][:, kc * 512:(kc + 1) * 512],
                                   self.d_probs[g * 512:(g + 1) * 512, kc * 128:(kc + 1) * 128])
            self.wait("tensor", f"pT{par}_s", t1)
            for hc in range(ND):
                i = g * ND + hc
                slot = i % 4
                pc = self.bank("A", slot)
                self.wait("tensor", "ve", ctx_free.get(slot, 0))
                def mm_c(en, par=par, hc=hc, pc=pc):
                    for kc in range(NT):
                        last = en.matmul(pc[:, :],
                                         self.vs_s[:, kc * D + hc * 128: kc * D + (hc + 1) * 128],
                                         pT[par][:, kc * 512:(kc + 1) * 512],
                                         start=(kc == 0), stop=(kc == NT - 1))
                    return last
                ttk = self.emit_tm(mm_c)
                pT_used[par] = ttk
                self.wait("vector", "te", ttk)
                self.wait("vector", f"c_cb{slot}", cb_dma.get(slot, 0))
                self.emit("vector", lambda en, slot=slot, pc=pc: en.tensor_copy(cb[slot][:, :], pc[:, :]))
                vtk = self.mark("vector", "ve")
                ctx_free[slot] = vtk
                self.wait("scalar", "ve", vtk)
                cb_dma[slot] = self.dma("scalar", f"c_cb{slot}",
                                        self.d_ctx[hc * 128:(hc + 1) * 128, g * 512:(g + 1) * 512],
                                        cb[slot][:, :])
            if g == 0:
                self.wait("scalar", "s_stgw", self.stage_done)
                for hc in range(ND):
                    self.dma("scalar", "c_wo", self.big[:, hc * D:(hc + 1) * D],
                             self.d_woutT[hc * 128:(hc + 1) * 128, :])
                self.wo_t = self.cnt["c_wo"]

    def p7_out(self, ctx):
        nc, NT, T = self.nc, self.NT, self.T
        sb = lambda n, sh, dt: ctx.enter_context(nc.sbuf_tensor(n, sh, dt))
        cp = [sb(f"cp{i}", [128, ND * 128], BF16) for i in range(2)]
        xrd = [sb(f"xrd{i}", [128, 1024], F32) for i in range(2)]
        mrd = [sb(f"mrd{i}", [128, 1024], F32) for i in range(2)]
        ob = [sb(f"ob{i}", [128, D], F32) for i in range(2)]
        tmp = sb("tmp", [128, 512], F32)
        boh = sb("boh", [1, D], BF16)
        bt = self.dma("gpsimd", "g_misc", boh[:, :], self.i_bout)
        self.wait("tensor", "c_wo", self.wo_t)
        self.wait("tensor", "g_misc", bt)
        self.wait("scalar", "g_acc0", self.moe_acc_t["g_acc0"])
        self.wait("scalar", "g_acc1", self.moe_acc_t["g_acc1"])
        cp_t = {}
        used = {}
        ob_dma = {}
        pe_free = {}
        xm_t = {}
        xm_used = {}
        def load_xm(tc, h):
            par = (tc * 2 + h) % 2
            self.wait("scalar", "ve", xm_used.get(par, 0))
            xt = self.dma("scalar", f"c_x{par}", xrd[par][:, :],
                          self.i_x[tc * 128:(tc + 1) * 128, h * 1024:(h + 1) * 1024])
            mt = self.dma("scalar", f"c_m{par}", mrd[par][:, :],
                          self.d_moe[tc * 128:(tc + 1) * 128, h * 1024:(h + 1) * 1024])
            xm_t[(tc, h)] = (par, xt, mt)
        load_xm(0, 0)
        load_xm(0, 1)
        for tc in range(NT):
            par = tc % 2
            self.wait("sync", "te", used.get(par, 0))
            cp_t[tc] = self.dma("sync", f"s_cp{par}",
                                cp[par][:, :].rearrange("p (hc f) -> p hc f", hc=ND),
                                self.d_ctx[:, tc * 128:(tc + 1) * 128]
                                .rearrange("(hc p) f -> p hc f", p=128))
            self.wait("tensor", f"s_cp{par}", cp_t[tc])
            for ot in range(4):
                i = tc * 4 + ot
                slot = i % 2
                h = ot // 2
                po = self.bank("B", slot)
                self.wait("tensor", "ve", pe_free.get(slot, 0))
                def mm_o(en, par=par, ot=ot, po=po):
                    for hc in range(ND):
                        en.matmul(po[:, :], cp[par][:, hc * 128:(hc + 1) * 128],
                                  self.big[:, hc * D + ot * 512: hc * D + (ot + 1) * 512],
                                  start=(hc == 0), stop=False)
                    return en.matmul(po[:, :], self.ones_s[0:1, 0:128],
                                     boh[0:1, ot * 512:(ot + 1) * 512], start=False, stop=True)
                ttk = self.emit_tm(mm_o)
                used[par] = ttk
                xpar, xt, mt = xm_t[(tc, h)]
                self.wait("vector", "te", ttk)
                self.wait("vector", f"c_x{xpar}", xt)
                self.wait("vector", f"c_m{xpar}", mt)
                self.wait("vector", f"s_ob{par}", ob_dma.get(par, 0))
                off = (ot % 2) * 512
                self.emit("vector", lambda en, xpar=xpar, ot=ot, po=po, par=par, off=off: (
                    en.tensor_add(tmp[:, :], po[:, :], xrd[xpar][:, off:off + 512]),
                    en.drain(),
                    en.tensor_add(ob[par][:, ot * 512:(ot + 1) * 512], tmp[:, :],
                                  mrd[xpar][:, off:off + 512])))
                vtk = self.mark("vector", "ve")
                pe_free[slot] = vtk
                if ot == 1:
                    xm_used[xpar] = vtk
                    if tc + 1 < NT:
                        load_xm(tc + 1, 0)
                elif ot == 3:
                    xm_used[xpar] = vtk
                    if tc + 1 < NT:
                        load_xm(tc + 1, 1)
            self.wait("scalar", "ve", self.cnt["ve"])
            ob_dma[par] = self.dma("scalar", f"s_ob{par}",
                                   self.o_out[tc * 128:(tc + 1) * 128, :], ob[par][:, :])

    def build(self):
        nc = self.nc
        with ExitStack() as ctx:
            self.ctx = ctx
            self.declare()
            self.alloc_persist(ctx)
            with ExitStack() as chm:
                self.ht = chm.enter_context(nc.sbuf_tensor("ht", [128, ND * self.T], BF16))
                with ExitStack() as cwb:
                    self.wdT = cwb.enter_context(nc.sbuf_tensor("wdT", [8, self.T], F32))
                    self.wb = cwb.enter_context(nc.sbuf_tensor("wb", [128, self.T], F32))
                    self.wb2 = cwb.enter_context(nc.sbuf_tensor("wb2", [128, self.T], F32))
                    with ExitStack() as c0:
                        self.p0_ln(c0)
                    self.barrier()
                    self.p1_ht()
                    with ExitStack() as c2:
                        self.p2_router(c2)
                        with ExitStack() as c3:
                            self.p3_moe(c3)
                            self.stage_weights(c3)
                            self.wait("sync", "te", self.moe_last_h)
                            for dc in range(ND):
                                self.dmat("xn_sync",
                                          self.ht[:, dc * self.T:(dc + 1) * self.T],
                                          self.d_xn[:, dc * 128:(dc + 1) * 128])
                            self.xn_t = self.cnt["xn_sync"]
                self.barrier()
                self.big = ctx.enter_context(
                    nc.sbuf_tensor("big", [128, ND * max(self.T, D)], BF16, side="right"))
                with ExitStack() as c4:
                    self.p4_qkv(c4)
                self.barrier()
            with ExitStack() as c56:
                self.vs_s = c56.enter_context(nc.sbuf_tensor("vs_s", [128, self.NT * D], BF16))
                self.pT = [c56.enter_context(nc.sbuf_tensor(f"pT{i}", [128, self.NT * 512], BF16))
                           for i in range(2)]
                with ExitStack() as c5:
                    self.p5_scores(c5)
                self.barrier()
                with ExitStack() as c6:
                    self.p6_ctx(c6)
                self.barrier()
            with ExitStack() as c7:
                self.p7_out(c7)
            self.barrier()
            block = ctx.enter_context(nc.Block())
            for e in ENGINES:
                def body(eng, e=e):
                    for fn in self.ops[e]:
                        fn(eng)
                getattr(block, e)(body)
        return nc


def build_nc(NT=16, dbg=False):
    nc = bass.Bass("TRN2", target_bir_lowering=False)
    b = Build(nc, NT=NT, dbg=dbg)
    b.build()
    return nc


def make_in_maps(inputs, NT=16):
    T = NT * 128
    x = np.ascontiguousarray(np.asarray(inputs["L0_attn_hidden_states"], dtype=np.float32))
    common = {
        "w_qkv": np.ascontiguousarray(np.asarray(inputs["L0_attn_w_qkv"], dtype=np.float32)),
        "b_qkv": np.ascontiguousarray(np.asarray(inputs["L0_attn_b_qkv"], dtype=np.float32).reshape(1, FQKV)),
        "w_out": np.ascontiguousarray(np.asarray(inputs["L0_attn_w_out"], dtype=np.float32)),
        "b_out": np.ascontiguousarray(np.asarray(inputs["L0_attn_b_out"], dtype=np.float32).reshape(1, D)),
        "rms_w": np.ascontiguousarray(np.asarray(inputs["L0_ffn_rms_weight"], dtype=np.float32).reshape(ND, 128)),
        "router_w": np.ascontiguousarray(np.asarray(inputs["L0_ffn_router_w"], dtype=np.float32)),
        "w_gate": np.ascontiguousarray(np.asarray(inputs["L0_ffn_w_gate"], dtype=np.float32)),
        "w_up": np.ascontiguousarray(np.asarray(inputs["L0_ffn_w_up"], dtype=np.float32)),
        "w_down": np.ascontiguousarray(np.asarray(inputs["L0_ffn_w_down"], dtype=np.float32)),
        "sw_gate": np.ascontiguousarray(np.asarray(inputs["L0_ffn_sw_gate"], dtype=np.float32)),
        "sw_up": np.ascontiguousarray(np.asarray(inputs["L0_ffn_sw_up"], dtype=np.float32)),
        "sw_down": np.ascontiguousarray(np.asarray(inputs["L0_ffn_sw_down"], dtype=np.float32)),
    }
    return [dict(common, x=np.ascontiguousarray(x[c][:T])) for c in range(x.shape[0])]


def kernel(**inputs):
    nc = build_nc(NT=16)
    in_maps = make_in_maps(inputs, NT=16)
    res = run_bass_kernel_spmd(nc, in_maps, core_ids=list(range(8)))
    return np.stack([res.results[c]["out"] for c in range(8)], axis=0)


# version 21
# speedup vs baseline: 1.1885x; 1.1885x over previous
import math
from contextlib import ExitStack

import numpy as np
import ml_dtypes

import concourse.bass as bass
import concourse.mybir as mybir
from concourse.bass_utils import run_bass_kernel_spmd

F32 = mybir.dt.float32
BF16 = mybir.dt.bfloat16
AF = mybir.ActivationFunctionType
ALU = mybir.AluOpType
AX = mybir.AxisListType

D = 2048
ND = 16
FQKV = 6144
NE = 8
FF = 1024
NFC = 8
EPS_LN = 1e-5
EPS_RMS = 1e-6
ENGINES = ("tensor", "vector", "scalar", "gpsimd", "sync")


class Build:

    H_PAIRS = 2
    J_BANKS = 4

    def __init__(self, nc, NT=16, dbg=False):
        self.nc = nc
        self.dbg = dbg
        self.NT = NT
        self.T = NT * 128
        assert self.T % 512 == 0
        self.NQG = self.T // 512
        self.ops = {e: [] for e in ENGINES}
        self.sems = {}
        self.cnt = {}
        self.nbar = 0

    def emit(self, eng, fn):
        self.ops[eng].append(fn)

    def sem(self, name):
        if name not in self.sems:
            self.sems[name] = self.ctx.enter_context(self.nc.semaphore(name))
            self.cnt[name] = 0
        return self.sems[name]

    def tick(self, name, n=1):
        self.sem(name)
        self.cnt[name] += n
        return self.cnt[name]

    def dma(self, qeng, sname, out, in_, **kw):
        s = self.sem(sname)
        tgt = self.tick(sname, 16)
        self.emit(qeng, lambda e, s=s, o=out, i=in_, kw=dict(kw):
                  e.dma_start(out=o, in_=i, **kw).then_inc(s, 16))
        return tgt

    def dmat(self, sname, out, in_, qeng="sync"):
        s = self.sem(sname)
        tgt = self.tick(sname, 16)
        self.emit(qeng, lambda e, s=s, o=out, i=in_:
                  e.dma_start_transpose(o, i).then_inc(s, 16))
        return tgt

    def wait(self, eng, sname, tgt):
        if tgt <= 0:
            return
        s = self.sem(sname)
        self.emit(eng, lambda e, s=s, t=tgt: e.wait_ge(s, t))

    def mark(self, eng, semname):
        s = self.sem(semname)
        tgt = self.tick(semname)
        if eng == "tensor":
            self.emit(eng, lambda e, s=s: e.nop().then_inc(s, 1))
        else:
            self.emit(eng, lambda e, s=s: (e.drain(), e.sem_inc(s, 1)))
        return tgt

    def emit_tm(self, fn, semname="te"):
        s = self.sem(semname)
        tgt = self.tick(semname)
        self.emit("tensor", lambda e, s=s, fn=fn: fn(e).then_inc(s, 1))
        return tgt

    def barrier(self):
        self.nbar += 1
        k = self.nbar
        snap = {nm: v for nm, v in self.cnt.items() if v > 0 and not nm.startswith("bar_")}
        for e in ENGINES:
            self.sem(f"bar_{e}")
            self.cnt[f"bar_{e}"] = k
        sems = self.sems
        for e in ENGINES:
            def f(eng, e=e, snap=dict(snap), k=k):
                if e in ("vector", "scalar"):
                    eng.drain()
                for nm, v in snap.items():
                    eng.wait_ge(sems[nm], v)
                eng.sem_inc(sems[f"bar_{e}"], 1)
                for e2 in ENGINES:
                    if e2 != e:
                        eng.wait_ge(sems[f"bar_{e2}"], k)
            self.emit(e, f)

    def declare(self):
        nc, T = self.nc, self.T
        self.i_x = nc.dram_tensor("x", [T, D], F32, kind="ExternalInput").ap()
        self.i_wqkv = nc.dram_tensor("w_qkv", [FQKV, D], F32, kind="ExternalInput").ap()
        self.i_bqkv = nc.dram_tensor("b_qkv", [1, FQKV], F32, kind="ExternalInput").ap()
        self.i_wout = nc.dram_tensor("w_out", [D, D], F32, kind="ExternalInput").ap()
        self.i_bout = nc.dram_tensor("b_out", [1, D], F32, kind="ExternalInput").ap()
        self.i_rmsw = nc.dram_tensor("rms_w", [ND, 128], F32, kind="ExternalInput").ap()
        self.i_router = nc.dram_tensor("router_w", [D, NE], F32, kind="ExternalInput").ap()
        self.i_wg = nc.dram_tensor("w_gate", [NE, D, FF], F32, kind="ExternalInput").ap()
        self.i_wu = nc.dram_tensor("w_up", [NE, D, FF], F32, kind="ExternalInput").ap()
        self.i_wd = nc.dram_tensor("w_down", [NE, FF, D], F32, kind="ExternalInput").ap()
        self.i_swg = nc.dram_tensor("sw_gate", [D, FF], F32, kind="ExternalInput").ap()
        self.i_swu = nc.dram_tensor("sw_up", [D, FF], F32, kind="ExternalInput").ap()
        self.i_swd = nc.dram_tensor("sw_down", [FF, D], F32, kind="ExternalInput").ap()
        self.o_out = nc.dram_tensor("out", [T, D], F32, kind="ExternalOutput").ap()
        kd = dict(kind="ExternalOutput") if self.dbg else {}
        self.d_xn = nc.dram_tensor("xn_d", [T, D], BF16, **kd).ap()
        self.d_wqkv_bf = nc.dram_tensor("wqkv_bf", [FQKV, D], BF16, **kd).ap()
        self.d_wqkvT = nc.dram_tensor("wqkvT", [D, FQKV], BF16, **kd).ap()
        self.d_wout_bf = nc.dram_tensor("wout_bf", [D, D], BF16).ap()
        self.d_woutT = nc.dram_tensor("woutT", [D, D], BF16).ap()
        self.d_q = nc.dram_tensor("q_d", [D, T], BF16, **kd).ap()
        self.d_v = nc.dram_tensor("v_d", [T, D], BF16, **kd).ap()
        self.d_probs = nc.dram_tensor("probs_d", [T, T], BF16, **kd).ap()
        self.d_ctx = nc.dram_tensor("ctx_d", [D, T], BF16, **kd).ap()
        self.d_moe = nc.dram_tensor("moe_d", [T, D], F32, **kd).ap()
        self.c_ones = nc.inline_tensor(np.ones((1, 512), dtype=ml_dtypes.bfloat16), "c_ones").ap()
        self.c_ident = nc.inline_tensor(np.eye(128, dtype=np.float32), "c_ident").ap()
        self.c_identb = nc.inline_tensor(np.eye(128, dtype=ml_dtypes.bfloat16), "c_identb").ap()
        basis = np.zeros((8, NE * 128), dtype=np.float32)
        for e in range(NE):
            basis[e, e * 128:(e + 1) * 128] = 1.0
        self.c_basis = nc.inline_tensor(basis, "c_basis").ap()

    def alloc_persist(self, ctx):
        nc, T = self.nc, self.T
        sb = lambda n, sh, dt: ctx.enter_context(nc.sbuf_tensor(n, sh, dt))
        self.ones_s = sb("ones_s", [1, 512], BF16)
        self.ident_s = sb("ident_s", [128, 128], F32)
        self.identb_s = sb("identb_s", [128, 128], BF16)
        self.basis_s = sb("basis_s", [8, NE * 128], F32)
        self.wfm = sb("wfm", [128, ND], F32)
        self.psA = ctx.enter_context(nc.psum_tensor("psA", [128, 2048], F32))
        self.psB = ctx.enter_context(nc.psum_tensor("psB", [128, 2048], F32))
        self.dma("gpsimd", "g_const", self.ones_s[:, :], self.c_ones)
        self.dma("gpsimd", "g_const", self.ident_s[:, :], self.c_ident)
        self.dma("gpsimd", "g_const", self.identb_s[:, :], self.c_identb)
        self.dma("gpsimd", "g_const", self.basis_s[:, :], self.c_basis)
        self.const_tgt = self.cnt["g_const"]

    def bank(self, which, i):
        t = self.psA if which == "A" else self.psB
        return t[:, i * 512:(i + 1) * 512]

    def p0_ln(self, ctx):
        nc, NT, T = self.nc, self.NT, self.T
        sb = lambda n, sh, dt: ctx.enter_context(nc.sbuf_tensor(n, sh, dt))
        W = 5
        xch = [sb(f"xch{i}", [128, D], F32) for i in range(W)]
        xnch = [sb(f"xnch{i}", [128, D], BF16) for i in range(W)]
        xn2ch = [sb(f"xn2ch{i}", [128, D], BF16) for i in range(W)]
        sq = [sb(f"sq{i}", [128, D], BF16) for i in range(W)]
        s2 = [sb(f"s2_{i}", [128, 1], F32) for i in range(W)]
        mean = [sb(f"mean{i}", [128, 1], F32) for i in range(W)]
        var = [sb(f"var{i}", [128, 1], F32) for i in range(W)]
        rstd = [sb(f"rstd{i}", [128, 1], F32) for i in range(W)]
        rs2 = [sb(f"rs2_{i}", [128, 2], F32) for i in range(W)]
        rsum = [sb(f"rsum{i}", [128, 1], F32) for i in range(W)]
        w16 = sb("w16", [16, 128], F32)
        wt = self.dma("sync", "s_misc", w16[:, :], self.i_rmsw)
        self.wait("tensor", "s_misc", wt)
        self.wait("tensor", "g_const", self.const_tgt)
        tt = self.emit_tm(lambda e: e.transpose(self.psB[0:128, 0:16], w16[:, :],
                                                self.ident_s[0:16, 0:16]))
        self.wait("vector", "te", tt)
        self.emit("vector", lambda e: e.tensor_copy(self.wfm[:, :], self.psB[0:128, 0:16]))
        self.wfm_ready = self.mark("vector", "ve")
        done_v = [0] * W
        done_a = [0] * W
        done_x = [0] * W
        tr_done = {}
        ev_done = {}
        xt = {}
        at_sq = {}
        vt_var = {}
        at_rs = {}
        vt_xn = {}
        at_sq2 = {}
        vt_rv = {}
        at_s2 = {}
        for w0 in range(0, NT, W):
            wave = range(w0, min(w0 + W, NT))
            for tc in wave:
                b = tc % W
                if done_v[b]:
                    self.wait("sync", "ve", done_v[b])
                    self.wait("sync", "ae", done_a[b])
                    self.wait("sync", "te", tr_done[tc - W])
                xt[tc] = self.dma("sync", f"s_x{b}", xch[b][:, :],
                                  self.i_x[tc * 128:(tc + 1) * 128, :])
            for tc in wave:
                b = tc % W
                self.wait("scalar", f"s_x{b}", xt[tc])
                self.emit("scalar", lambda e: e.drain())
                self.emit("scalar", lambda e, b=b: e.activation(
                    sq[b][:, :], xch[b][:, :], AF.Square, accum_out=s2[b][:, :]))
                at_sq[tc] = self.mark("scalar", "ae")
            for tc in wave:
                b = tc % W
                self.wait("vector", f"s_x{b}", xt[tc])
                self.emit("vector", lambda e, b=b: (
                    e.reduce_sum(mean[b][:, :], xch[b][:, :], axis=AX.X), e.drain(),
                    e.tensor_scalar(mean[b][:, :], mean[b][:, :], 1.0 / D, None, op0=ALU.mult)))
                self.mark("vector", "ve")
            for tc in wave:
                b = tc % W
                self.wait("vector", "ae", at_sq[tc])
                self.emit("vector", lambda e, b=b: (
                    e.tensor_mul(var[b][:, :], mean[b][:, :], mean[b][:, :]), e.drain(),
                    e.tensor_scalar(s2[b][:, :], s2[b][:, :], 1.0 / D, EPS_LN,
                                    op0=ALU.mult, op1=ALU.add), e.drain(),
                    e.tensor_sub(rs2[b][:, 0:1], s2[b][:, :], var[b][:, :]), e.drain(),
                    e.reciprocal(rsum[b][:, :], rs2[b][:, 0:1]), e.drain(),
                    e.tensor_scalar(var[b][:, :], rs2[b][:, 0:1], -EPS_LN, None, op0=ALU.add),
                    e.drain(),
                    e.tensor_mul(var[b][:, :], var[b][:, :], rsum[b][:, :]), e.drain(),
                    e.tensor_scalar(rs2[b][:, 1:2], var[b][:, :], 1.0, EPS_RMS,
                                    op0=ALU.mult, op1=ALU.add)))
                vt_var[tc] = self.mark("vector", "ve")
            for tc in wave:
                b = tc % W
                self.wait("scalar", "ve", vt_var[tc])
                self.emit("scalar", lambda e: e.drain())
                self.emit("scalar", lambda e, b=b: e.activation(rs2[b][:, :], rs2[b][:, :], AF.Sqrt))
                at_rs[tc] = self.mark("scalar", "ae")
            for tc in wave:
                b = tc % W
                self.wait("vector", "ae", at_rs[tc])
                if done_x[b]:
                    self.wait("vector", f"s_xn{b}", done_x[b])
                if tc >= W:
                    self.wait("vector", "te", tr_done[tc - W])
                self.emit("vector", lambda e, b=b: (
                    e.reciprocal(rstd[b][:, :], rs2[b][:, 0:1]), e.drain(),
                    e.reciprocal(rsum[b][:, :], rs2[b][:, 1:2]), e.drain(),
                    e.tensor_mul(rsum[b][:, :], rsum[b][:, :], rstd[b][:, :]), e.drain(),
                    e.tensor_scalar(xnch[b][:, :], xch[b][:, :], mean[b][:, :], rstd[b][:, :],
                                    op0=ALU.subtract, op1=ALU.mult), e.drain(),
                    e.tensor_scalar(xn2ch[b][:, :], xch[b][:, :], mean[b][:, :], rsum[b][:, :],
                                    op0=ALU.subtract, op1=ALU.mult)))
                done_v[b] = self.mark("vector", "ve")
                done_a[b] = at_sq[tc]
                self.wait("sync", "ve", done_v[b])
                self.dma("sync", f"s_xn{b}", self.d_xn[tc * 128:(tc + 1) * 128, :], xnch[b][:, :])
                done_x[b] = self.cnt[f"s_xn{b}"]
            for tc in wave:
                b = tc % W
                self.wait("tensor", "ve", done_v[b])
                for half in range(2):
                    bank = (tc % 2) * 2 + half
                    pv = self.psA[:, bank * 512:(bank + 1) * 512].bitcast(BF16)
                    self.wait("tensor", "ve", ev_done.get((tc - 2, half), 0))
                    def trg(e, b=b, half=half, pv=pv):
                        for j in range(8):
                            dc = half * 8 + j
                            last = e.transpose(pv[:, j * 128:(j + 1) * 128],
                                               xn2ch[b][:, dc * 128:(dc + 1) * 128],
                                               self.identb_s[:, :])
                        return last
                    ttk = self.emit_tm(trg)
                    tr_done[tc] = ttk
                    self.wait("vector", "te", ttk)
                    def evac(e, tc=tc, half=half, pv=pv):
                        e.tensor_copy(
                            self.ht[:, 0:ND * T]
                            .rearrange("p (dc t) -> p dc t", dc=ND)[:, half * 8:(half + 1) * 8,
                                                                   tc * 128:(tc + 1) * 128],
                            pv[:, :].rearrange("p (dc t) -> p dc t", dc=8))
                    self.emit("vector", evac)
                    ev_done[(tc, half)] = self.mark("vector", "ve")
        self.ln_ev_done = ev_done
        self.ln_tr_done = tr_done

    def p1_ht(self):
        T = self.T
        self.wait("vector", "ve", self.wfm_ready)
        self.wait("vector", "ve", self.ln_ev_done[(self.NT - 1, 1)])
        for dc in range(ND):
            self.emit("vector", lambda e, dc=dc: e.tensor_scalar(
                self.ht[:, dc * T:(dc + 1) * T], self.ht[:, dc * T:(dc + 1) * T],
                self.wfm[:, dc:dc + 1], None, op0=ALU.mult))
            self.mark("vector", "ve")
        self.ht_ready = self.cnt["ve"]

    def p2_router(self, ctx):
        nc, NT, T = self.nc, self.NT, self.T
        sb = lambda n, sh, dt: ctx.enter_context(nc.sbuf_tensor(n, sh, dt))
        wr = sb("wr", [128, ND * NE], BF16)
        probs8 = sb("probs8", [128, NE], F32)
        maxs8 = sb("maxs8", [128, NE], F32)
        sel8 = sb("sel8", [128, NE], F32)
        wd_all = sb("wd_all", [128, NE], F32)
        r1 = sb("r1", [128, 1], F32)
        r2 = sb("r2", [128, 1], F32)
        r3 = sb("r3", [128, 1], F32)
        wrt = self.dma("gpsimd", "g_wr", wr[:, :].rearrange("p (dc e) -> p dc e", dc=ND),
                       self.i_router.rearrange("(dc p) e -> p dc e", p=128))
        self.wait("tensor", "g_wr", wrt)
        self.wait("tensor", "ve", self.ht_ready)
        rt_done = {}
        for tc in range(NT):
            pb = self.bank("A", tc % 2)
            if tc >= 2:
                self.wait("tensor", "ve", rt_done[tc - 2])
            def mm_router(e, tc=tc, pb=pb):
                for dc in range(ND):
                    last = e.matmul(pb[:, 0:NE],
                                    self.ht[:, dc * T + tc * 128: dc * T + (tc + 1) * 128],
                                    wr[:, dc * NE:(dc + 1) * NE],
                                    start=(dc == 0), stop=(dc == ND - 1))
                return last
            tt = self.emit_tm(mm_router)
            self.wait("vector", "te", tt)
            self.emit("vector", lambda e, pb=pb: (
                e.reduce_max(r1[:, :], pb[:, 0:NE], axis=AX.X), e.drain(),
                e.tensor_scalar(r1[:, :], r1[:, :], -1.0, None, op0=ALU.mult)))
            vt = self.mark("vector", "ve")
            self.wait("scalar", "ve", vt)
            self.emit("scalar", lambda e: e.drain())
            self.emit("scalar", lambda e, pb=pb: e.activation(
                probs8[:, :], pb[:, 0:NE], AF.Exp, bias=r1[:, :], accum_out=r2[:, :]))
            at = self.mark("scalar", "ae")
            self.wait("vector", "ae", at)
            self.emit("vector", lambda e: (
                e.reciprocal(r3[:, :], r2[:, :]), e.drain(),
                e.tensor_scalar(probs8[:, :], probs8[:, :], r3[:, :], None, op0=ALU.mult), e.drain(),
                e.max(out=maxs8[:, :], in_=probs8[:, :]), e.drain(),
                e.tensor_scalar(sel8[:, :], probs8[:, :], maxs8[:, 1:2], None, op0=ALU.is_ge),
                e.drain(),
                e.tensor_add(r1[:, :], maxs8[:, 0:1], maxs8[:, 1:2]), e.drain(),
                e.reciprocal(r2[:, :], r1[:, :]), e.drain(),
                e.scalar_tensor_tensor(wd_all[:, :], probs8[:, :], r2[:, :], sel8[:, :],
                                       op0=ALU.mult, op1=ALU.mult)))
            vt2 = self.mark("vector", "ve")
            self.wait("tensor", "ve", vt2)
            tt2 = self.emit_tm(lambda e, tc=tc: e.transpose(
                self.psB[0:8, (tc % 2) * 512:(tc % 2) * 512 + 128], wd_all[:, :],
                self.ident_s[:, :]))
            self.wait("vector", "te", tt2)
            self.emit("vector", lambda e, tc=tc: e.tensor_copy(
                self.wdT[0:8, tc * 128:(tc + 1) * 128],
                self.psB[0:8, (tc % 2) * 512:(tc % 2) * 512 + 128]))
            rt_done[tc] = self.mark("vector", "ve")
        self.wd_ready = self.cnt["ve"]

    def p3_moe(self, ctx):
        nc, NT, T, NQG = self.nc, self.NT, self.T, self.NQG
        sb = lambda n, sh, dt: ctx.enter_context(nc.sbuf_tensor(n, sh, dt))
        wgb = [sb(f"wgb{i}", [128, ND * 128], BF16) for i in range(2)]
        wub = [sb(f"wub{i}", [128, ND * 128], BF16) for i in range(2)]
        wdb = [sb(f"wdb{i}", [128, NFC * 1024], BF16) for i in range(2)]
        a_s = sb("a_s", [128, NFC * T], BF16)
        silu_b = [sb(f"silu{i}", [128, 512], BF16) for i in range(2)]
        tmul_b = [sb(f"tmul{i}", [128, 512], BF16) for i in range(2)]
        jbuf = [sb(f"jbuf{i}", [128, 2048], F32) for i in range(3)]

        wg_t = {}; wu_t = {}; wd_t = {}
        def prefetch_gu(e, fc, parity, after_te=0):
            self.wait("gpsimd", "te", after_te)
            src_g = self.i_wg[e, :, :] if e < NE else self.i_swg
            src_u = self.i_wu[e, :, :] if e < NE else self.i_swu
            wg_t[(e, fc)] = self.dma(
                "gpsimd", f"g_wg{parity}",
                wgb[parity][:, :].rearrange("p (dc f) -> p dc f", dc=ND),
                src_g[:, fc * 128:(fc + 1) * 128].rearrange("(dc p) f -> p dc f", p=128))
            wu_t[(e, fc)] = self.dma(
                "gpsimd", f"g_wu{parity}",
                wub[parity][:, :].rearrange("p (dc f) -> p dc f", dc=ND),
                src_u[:, fc * 128:(fc + 1) * 128].rearrange("(dc p) f -> p dc f", p=128))

        def prefetch_wd(e, half, parity, after_te=0):
            self.wait("gpsimd", "te", after_te)
            src = self.i_wd[e, :, :] if e < NE else self.i_swd
            wd_t[(e, half)] = self.dma(
                "gpsimd", f"g_wd{parity}",
                wdb[parity][:, :].rearrange("p (fc d) -> p fc d", fc=NFC),
                src[:, half * 1024:(half + 1) * 1024].rearrange("(fc p) d -> p fc d", p=128))

        prefetch_gu(0, 0, 0)
        prefetch_gu(0, 1, 1)
        prefetch_wd(0, 0, 0)
        slabs = self.staging_cast_slabs()
        def bg_slab(n=1):
            for _ in range(n):
                if slabs:
                    slabs.pop(0)()

        self.wait("tensor", "ve", self.wd_ready)
        gu_mm_done = {}
        wd_mm_done = {}
        wb_free = {}
        j_free = {}
        h_free = {}
        jb_dma = {}
        wb_ready_t = {}
        wb_buf = [self.wb[:, 0:T], self.wb2[:, 0:T]]

        def wb_build(e, h_free=None):
            dst = wb_buf[e % 2]
            ttks = []
            for tt in range(NQG):
                pb = self.bank("A", tt % 4)
                self.wait("tensor", "ve", wb_free.get(tt % 4, 0))
                if h_free:
                    self.wait("tensor", "ve", h_free.get((tt % 4) // 2, 0))
                    self.wait("tensor", "ve", h_free.get(2, 0))
                ttks.append(self.emit_tm(lambda en, e=e, tt=tt, pb=pb: en.matmul(
                    pb[:, :], self.basis_s[:, e * 128:(e + 1) * 128],
                    self.wdT[0:8, tt * 512:(tt + 1) * 512], start=True, stop=True)))
            for tt in range(NQG):
                pb = self.bank("A", tt % 4)
                self.wait("vector", "te", ttks[tt])
                self.emit("vector", lambda en, tt=tt, pb=pb, dst=dst: en.tensor_copy(
                    dst[:, tt * 512:(tt + 1) * 512], pb[:, :]))
                wb_free[tt % 4] = self.mark("vector", "ve")
            wb_ready_t[e] = self.cnt["ve"]

        wb_build(0)
        for e in range(NE + 1):
            wb_ready = wb_ready_t.get(e, 0)

            a_done_v = 0
            pairs = [("A", 0), ("A", 2), ("B", 0)][:self.H_PAIRS]
            for fc in range(NFC):
                par = fc % 2
                for tt in range(NQG):
                    slot = (fc * NQG + tt) % self.H_PAIRS
                    w, b0 = pairs[slot]
                    pg = self.bank(w, b0)
                    pu = self.bank(w, b0 + 1)
                    self.wait("tensor", f"g_wg{par}", wg_t[(e, fc)])
                    self.wait("tensor", f"g_wu{par}", wu_t[(e, fc)])
                    if wb_ready and fc == 0 and tt == 0:
                        self.wait("tensor", "ve", wb_ready)
                    self.wait("tensor", "ve", h_free.get(slot, 0))
                    if w == "B":
                        self.wait("tensor", "ve", j_free.get(b0, 0))
                        self.wait("tensor", "ve", j_free.get(b0 + 1, 0))
                    def mm_g(en, tt=tt, par=par, pg=pg):
                        for dc in range(ND):
                            last = en.matmul(pg[:, :],
                                             wgb[par][:, dc * 128:(dc + 1) * 128],
                                             self.ht[:, dc * T + tt * 512: dc * T + (tt + 1) * 512],
                                             start=(dc == 0), stop=(dc == ND - 1))
                        return last
                    tg = self.emit_tm(mm_g)
                    def mm_u(en, tt=tt, par=par, pu=pu):
                        for dc in range(ND):
                            last = en.matmul(pu[:, :],
                                             wub[par][:, dc * 128:(dc + 1) * 128],
                                             self.ht[:, dc * T + tt * 512: dc * T + (tt + 1) * 512],
                                             start=(dc == 0), stop=(dc == ND - 1))
                        return last
                    tu = self.emit_tm(mm_u)
                    gu_mm_done[(e, fc)] = tu
                    self.wait("scalar", "te", tg)
                    self.wait("scalar", "ve", h_free.get(slot, 0))
                    self.emit("scalar", lambda en: en.drain())
                    self.emit("scalar", lambda en, slot=slot, pg=pg: en.activation(
                        silu_b[slot][:, :], pg[:, :], AF.Silu))
                    atk = self.mark("scalar", "ae")
                    self.wait("vector", "te", tu)
                    if e < NE:
                        self.emit("vector", lambda en, e=e, tt=tt, slot=slot, pu=pu:
                                  en.tensor_mul(tmul_b[slot][:, :], pu[:, :],
                                                wb_buf[e % 2][:, tt * 512:(tt + 1) * 512]))
                        self.emit("vector", lambda en: en.drain())
                        self.wait("vector", "ae", atk)
                        self.emit("vector", lambda en, fc=fc, tt=tt, slot=slot:
                                  en.tensor_mul(a_s[:, fc * T + tt * 512: fc * T + (tt + 1) * 512],
                                                silu_b[slot][:, :], tmul_b[slot][:, :]))
                    else:
                        self.wait("vector", "ae", atk)
                        self.emit("vector", lambda en, fc=fc, tt=tt, slot=slot, pu=pu:
                                  en.tensor_mul(a_s[:, fc * T + tt * 512: fc * T + (tt + 1) * 512],
                                                silu_b[slot][:, :], pu[:, :]))
                    vtk = self.mark("vector", "ve")
                    h_free[slot] = vtk
                    a_done_v = vtk
                if fc + 2 < NFC:
                    prefetch_gu(e, fc + 2, par, after_te=gu_mm_done[(e, fc)])
                elif e < NE:
                    prefetch_gu(e + 1, fc + 2 - NFC, par, after_te=gu_mm_done[(e, fc)])
                bg_slab(1 if e == 0 else 1)

            for half in range(2):
                parj = half
                if half == 0:
                    prefetch_wd(e, 1, 1, after_te=wd_mm_done.get((e - 1, 1), 0))
                elif e < NE:
                    prefetch_wd(e + 1, 0, 0, after_te=wd_mm_done.get((e, 0), 0))
                if half == 1 and e + 1 < NE:
                    wb_build(e + 1, h_free=h_free)
                self.wait("tensor", f"g_wd{parj}", wd_t[(e, half)])
                self.wait("tensor", "ve", a_done_v)
                for tc in range(NT):
                    slotj = (tc // 2) % 3
                    sub = tc % 2
                    for dt in range(2):
                        jb = (tc * 2 + dt) % 4 if self.J_BANKS == 4 else 2 + dt
                        pj = self.bank("B", jb)
                        self.wait("tensor", "ve", j_free.get(jb, 0))
                        if jb < 2:
                            self.wait("tensor", "ve", h_free.get(2, 0))
                        def mm_j(en, tc=tc, dt=dt, parj=parj, pj=pj):
                            for fc in range(NFC):
                                last = en.matmul(pj[:, :],
                                                 a_s[:, fc * T + tc * 128: fc * T + (tc + 1) * 128],
                                                 wdb[parj][:, fc * 1024 + dt * 512: fc * 1024 + (dt + 1) * 512],
                                                 start=(fc == 0), stop=(fc == NFC - 1))
                            return last
                        ttk = self.emit_tm(mm_j)
                        wd_mm_done[(e, half)] = ttk
                        self.wait("vector", "te", ttk)
                        self.wait("vector", f"g_acc{slotj}", jb_dma.get(slotj, 0))
                        self.emit("vector", lambda en, slotj=slotj, sub=sub, dt=dt, pj=pj:
                                  en.tensor_copy(jbuf[slotj][:, sub * 1024 + dt * 512:
                                                             sub * 1024 + (dt + 1) * 512], pj[:, :]))
                        j_free[jb] = self.mark("vector", "ve")
                    if sub == 1:
                        self.wait("gpsimd", "ve", self.cnt["ve"])
                        kw = {} if e == 0 else {"accum_op": ALU.add}
                        jb_dma[slotj] = self.dma(
                            "gpsimd", f"g_acc{slotj}",
                            self.d_moe[(tc - 1) * 128:(tc + 1) * 128,
                                       half * 1024:(half + 1) * 1024]
                            .rearrange("(two p) d -> p two d", p=128),
                            jbuf[slotj][:, :].rearrange("p (two d) -> p two d", two=2), **kw)
        bg_slab(len(slabs))
        self.moe_last_h = gu_mm_done[(NE, NFC - 1)]
        self.moe_acc_t = {k: self.cnt.get(k, 0) for k in ("g_acc0", "g_acc1")}

    def staging_cast_slabs(self):
        slabs = []
        for r in range(0, FQKV, 512):
            slabs.append(lambda r=r: self.dma("gpsimd", "g_stgc",
                                              self.d_wqkv_bf[r:r + 512, :],
                                              self.i_wqkv[r:r + 512, :]))
        for r in range(0, D, 512):
            slabs.append(lambda r=r: self.dma("gpsimd", "g_stgc",
                                              self.d_wout_bf[r:r + 512, :],
                                              self.i_wout[r:r + 512, :]))
        return slabs

    def stage_weights(self, ctx):
        nc = self.nc
        HB = FQKV // 3
        bounce = ctx.enter_context(nc.sbuf_tensor("stg", [128, HB], BF16))
        cast_done = self.cnt.get("g_stgc", 0)
        self.wait("sync", "g_stgc", cast_done)
        prev = 0
        for dc in range(ND):
            for fh in range(3):
                self.wait("sync", "s_stgw", prev)
                t = self.dmat("s_stgt", bounce[:, :],
                              self.d_wqkv_bf[fh * HB:(fh + 1) * HB, dc * 128:(dc + 1) * 128])
                self.wait("sync", "s_stgt", t)
                prev = self.dma("sync", "s_stgw",
                                self.d_wqkvT[dc * 128:(dc + 1) * 128, fh * HB:(fh + 1) * HB],
                                bounce[:, :])
        for dc in range(ND):
            self.wait("sync", "s_stgw", prev)
            t = self.dmat("s_stgt", bounce[:, 0:D], self.d_wout_bf[:, dc * 128:(dc + 1) * 128])
            self.wait("sync", "s_stgt", t)
            prev = self.dma("sync", "s_stgw", self.d_woutT[dc * 128:(dc + 1) * 128, :],
                            bounce[:, 0:D])
        self.stage_done = self.cnt["s_stgw"]

    def p4_qkv(self, ctx):
        nc, NT, T, NQG = self.nc, self.NT, self.T, self.NQG
        sb = lambda n, sh, dt: ctx.enter_context(nc.sbuf_tensor(n, sh, dt))
        wqp = [sb(f"wqp{i}", [128, ND * 128], BF16) for i in range(2)]
        vp = [sb(f"vp{i}", [128, ND * 512], BF16) for i in range(2)]
        evb = [sb(f"evb{i}", [128, 512], BF16) for i in range(4)]
        bqh = sb("bqh", [1, FQKV], BF16)
        bt = self.dma("gpsimd", "g_misc", bqh[:, :], self.i_bqkv)
        self.wait("tensor", "g_misc", bt)
        self.wait("tensor", "xn_sync", self.xn_t)
        piece_t = {}
        par_done = {}
        qkv_free = {}
        ev_dma = {}
        for i, fc in enumerate(range(0, 32)):
            par = i % 2
            self.wait("sync", "te", par_done.get(par, 0))
            piece_t[fc] = self.dma("sync", f"s_wqp{par}",
                                   wqp[par][:, :].rearrange("p (dc f) -> p dc f", dc=ND),
                                   self.d_wqkvT[:, fc * 128:(fc + 1) * 128]
                                   .rearrange("(dc p) f -> p dc f", p=128))
            self.wait("tensor", f"s_wqp{par}", piece_t[fc])
            for tt in range(NQG):
                slot = (i * NQG + tt) % 4
                pq = self.bank("A", slot)
                self.wait("tensor", "ve", qkv_free.get(slot, 0))
                def mm_qk(en, fc=fc, tt=tt, par=par, pq=pq):
                    for dc in range(ND):
                        en.matmul(pq[:, :], wqp[par][:, dc * 128:(dc + 1) * 128],
                                  self.ht[:, dc * T + tt * 512: dc * T + (tt + 1) * 512],
                                  start=(dc == 0), stop=False)
                    return en.matmul(pq[:, :], bqh[0:1, fc * 128:(fc + 1) * 128],
                                     self.ones_s[0:1, :], start=False, stop=True)
                ttk = self.emit_tm(mm_qk)
                par_done[par] = ttk
                self.wait("vector", "te", ttk)
                if fc < 16:
                    eb = evb[slot]
                    self.wait("vector", f"c_ev{slot}", ev_dma.get(slot, 0))
                    self.emit("vector", lambda en, eb=eb, pq=pq: en.tensor_copy(eb[:, :], pq[:, :]))
                    vtk = self.mark("vector", "ve")
                    self.wait("scalar", "ve", vtk)
                    ev_dma[slot] = self.dma(
                        "scalar", f"c_ev{slot}",
                        self.d_q[fc * 128:(fc + 1) * 128, tt * 512:(tt + 1) * 512], eb[:, :])
                else:
                    hc = fc - 16
                    self.emit("vector", lambda en, hc=hc, tt=tt, pq=pq: en.tensor_copy(
                        self.big[:, hc * T + tt * 512: hc * T + (tt + 1) * 512], pq[:, :]))
                    vtk = self.mark("vector", "ve")
                qkv_free[slot] = vtk
        vp_used = {}
        for hti in range(4):
            vpar = hti % 2
            self.wait("sync", "te", vp_used.get(vpar, 0))
            vt = self.dma("sync", f"s_vp{vpar}",
                          vp[vpar][:, :].rearrange("p (dc f) -> p dc f", dc=ND),
                          self.d_wqkvT[:, 4096 + hti * 512: 4096 + (hti + 1) * 512]
                          .rearrange("(dc p) f -> p dc f", p=128))
            self.wait("tensor", f"s_vp{vpar}", vt)
            for tc in range(NT):
                slot = (hti * NT + tc) % 4
                pv = self.bank("A", slot)
                self.wait("tensor", "ve", qkv_free.get(slot, 0))
                def mm_v(en, hti=hti, tc=tc, pv=pv, vpar=vpar):
                    for dc in range(ND):
                        en.matmul(pv[:, :], self.ht[:, dc * T + tc * 128: dc * T + (tc + 1) * 128],
                                  vp[vpar][:, dc * 512:(dc + 1) * 512],
                                  start=(dc == 0), stop=False)
                    return en.matmul(pv[:, :], self.ones_s[0:1, 0:128],
                                     bqh[0:1, 4096 + hti * 512: 4096 + (hti + 1) * 512],
                                     start=False, stop=True)
                ttk = self.emit_tm(mm_v)
                vp_used[vpar] = ttk
                self.wait("vector", "te", ttk)
                eb = evb[slot]
                self.wait("vector", f"c_ev{slot}", ev_dma.get(slot, 0))
                self.emit("vector", lambda en, eb=eb, pv=pv: en.tensor_copy(eb[:, :], pv[:, :]))
                vtk = self.mark("vector", "ve")
                self.wait("scalar", "ve", vtk)
                ev_dma[slot] = self.dma(
                    "scalar", f"c_ev{slot}",
                    self.d_v[tc * 128:(tc + 1) * 128, hti * 512:(hti + 1) * 512], eb[:, :])
                qkv_free[slot] = vtk

    def p5_scores(self, ctx):
        nc, NT, T, NQG = self.nc, self.NT, self.T, self.NQG
        sb = lambda n, sh, dt: ctx.enter_context(nc.sbuf_tensor(n, sh, dt))
        qg = [sb(f"qg{i}", [128, ND * 512], BF16) for i in range(2)]
        prb = [sb(f"prb{i}", [128, T], BF16) for i in range(2)]
        rm = sb("rm", [128, 1], F32)
        se = sb("se", [128, 1], F32)
        ri = sb("ri", [128, 1], F32)
        SCALE = 1.0 / math.sqrt(float(D))
        self.pT_pre = {}
        qg_used = {}
        exp_done = {}
        pr_dma = {}
        q_t = {}
        def load_q(g):
            par = g % 2
            self.wait("gpsimd", "te", qg_used.get(par, 0))
            q_t[g] = self.dma("gpsimd", f"g_qg{par}",
                              qg[par][:, :].rearrange("p (hc q) -> p hc q", hc=ND),
                              self.d_q[:, g * 512:(g + 1) * 512]
                              .rearrange("(hc p) q -> p hc q", p=128))
        load_q(0)
        for g in range(NQG):
            par = g % 2
            if g + 1 < NQG:
                load_q(g + 1)
            for kc in range(g * NT // NQG, (g + 1) * NT // NQG):
                self.dma("gpsimd", "g_vload", self.vs_s[:, kc * D:(kc + 1) * D],
                         self.d_v[kc * 128:(kc + 1) * 128, :])
            self.wait("tensor", f"g_qg{par}", q_t[g])
            for qc in range(4):
                i = g * 4 + qc
                ps_full = self.psA if i % 2 == 0 else self.psB
                self.wait("tensor", "ae", exp_done.get(i - 2, 0))
                def mm_s(en, par=par, qc=qc, ps_full=ps_full):
                    for kt in range(NQG):
                        for hc in range(ND):
                            last = en.matmul(ps_full[:, kt * 512:(kt + 1) * 512],
                                             qg[par][:, hc * 512 + qc * 128: hc * 512 + (qc + 1) * 128],
                                             self.big[:, hc * T + kt * 512: hc * T + (kt + 1) * 512],
                                             start=(hc == 0), stop=(hc == ND - 1))
                    return last
                ttk = self.emit_tm(mm_s)
                qg_used[par] = ttk
                self.wait("vector", "te", ttk)
                self.emit("vector", lambda en, ps_full=ps_full: (
                    en.reduce_max(rm[:, :], ps_full[:, 0:T], axis=AX.X), en.drain(),
                    en.tensor_scalar(rm[:, :], rm[:, :], -SCALE, None, op0=ALU.mult)))
                vtk = self.mark("vector", "ve")
                bpr = i % 2
                self.wait("scalar", f"s_pr{bpr}", pr_dma.get(bpr, 0))
                self.wait("scalar", "ve", vtk)
                self.emit("scalar", lambda en: en.drain())
                self.emit("scalar", lambda en, ps_full=ps_full, bpr=bpr: en.activation(
                    prb[bpr][:, :], ps_full[:, 0:T], AF.Exp, bias=rm[:, :], scale=SCALE,
                    accum_out=se[:, :]))
                atk = self.mark("scalar", "ae")
                exp_done[i] = atk
                self.wait("vector", "ae", atk)
                self.emit("vector", lambda en, bpr=bpr: (
                    en.reciprocal(ri[:, :], se[:, :]), en.drain(),
                    en.tensor_scalar(prb[bpr][:, :], prb[bpr][:, :], ri[:, :], None, op0=ALU.mult)))
                vtk2 = self.mark("vector", "ve")
                self.wait("sync", "ve", vtk2)
                row = g * 512 + qc * 128
                pr_dma[bpr] = self.dma("sync", f"s_pr{bpr}",
                                       self.d_probs[row:row + 128, :], prb[bpr][:, :])
            if g < 2:
                self.wait("sync", "s_pr0", self.cnt.get("s_pr0", 0))
                self.wait("sync", "s_pr1", self.cnt.get("s_pr1", 0))
                par = g % 2
                t1 = 0
                for kc in range(NT):
                    t1 = self.dmat(f"pT{par}_s", self.pT[par][:, kc * 512:(kc + 1) * 512],
                                   self.d_probs[g * 512:(g + 1) * 512, kc * 128:(kc + 1) * 128])
                self.pT_pre[g] = t1
        self.v_t = self.cnt["g_vload"]

    def p6_ctx(self, ctx):
        nc, NT, T, NQG = self.nc, self.NT, self.T, self.NQG
        sb = lambda n, sh, dt: ctx.enter_context(nc.sbuf_tensor(n, sh, dt))
        pT = self.pT
        cb = [sb(f"cb{i}", [128, 512], BF16) for i in range(4)]
        self.wait("tensor", "g_vload", self.v_t)
        pT_used = {}
        cb_dma = {}
        ctx_free = {}
        for g in range(NQG):
            par = g % 2
            if g in self.pT_pre:
                t1 = self.pT_pre[g]
            else:
                self.wait("sync", "te", pT_used.get(par, 0))
                t1 = 0
                for kc in range(NT):
                    t1 = self.dmat(f"pT{par}_s", pT[par][:, kc * 512:(kc + 1) * 512],
                                   self.d_probs[g * 512:(g + 1) * 512, kc * 128:(kc + 1) * 128])
            self.wait("tensor", f"pT{par}_s", t1)
            for hc in range(ND):
                i = g * ND + hc
                slot = i % 4
                pc = self.bank("A", slot)
                self.wait("tensor", "ve", ctx_free.get(slot, 0))
                def mm_c(en, par=par, hc=hc, pc=pc):
                    for kc in range(NT):
                        last = en.matmul(pc[:, :],
                                         self.vs_s[:, kc * D + hc * 128: kc * D + (hc + 1) * 128],
                                         pT[par][:, kc * 512:(kc + 1) * 512],
                                         start=(kc == 0), stop=(kc == NT - 1))
                    return last
                ttk = self.emit_tm(mm_c)
                pT_used[par] = ttk
                self.wait("vector", "te", ttk)
                self.wait("vector", f"c_cb{slot}", cb_dma.get(slot, 0))
                self.emit("vector", lambda en, slot=slot, pc=pc: en.tensor_copy(cb[slot][:, :], pc[:, :]))
                vtk = self.mark("vector", "ve")
                ctx_free[slot] = vtk
                self.wait("scalar", "ve", vtk)
                cb_dma[slot] = self.dma("scalar", f"c_cb{slot}",
                                        self.d_ctx[hc * 128:(hc + 1) * 128, g * 512:(g + 1) * 512],
                                        cb[slot][:, :])
            if g == 0:
                self.wait("scalar", "s_stgw", self.stage_done)
                for hc in range(ND):
                    self.dma("scalar", "c_wo", self.big[:, hc * D:(hc + 1) * D],
                             self.d_woutT[hc * 128:(hc + 1) * 128, :])
                self.wo_t = self.cnt["c_wo"]

    def p7_out(self, ctx):
        nc, NT, T = self.nc, self.NT, self.T
        sb = lambda n, sh, dt: ctx.enter_context(nc.sbuf_tensor(n, sh, dt))
        cp = [sb(f"cp{i}", [128, ND * 128], BF16) for i in range(2)]
        xrd = [sb(f"xrd{i}", [128, 1024], F32) for i in range(2)]
        mrd = [sb(f"mrd{i}", [128, 1024], F32) for i in range(2)]
        ob = [sb(f"ob{i}", [128, D], F32) for i in range(2)]
        tmp = sb("tmp", [128, 512], F32)
        boh = sb("boh", [1, D], BF16)
        bt = self.dma("gpsimd", "g_misc", boh[:, :], self.i_bout)
        self.wait("tensor", "c_wo", self.wo_t)
        self.wait("tensor", "g_misc", bt)
        self.wait("scalar", "g_acc0", self.moe_acc_t["g_acc0"])
        self.wait("scalar", "g_acc1", self.moe_acc_t["g_acc1"])
        cp_t = {}
        used = {}
        ob_dma = {}
        pe_free = {}
        xm_t = {}
        xm_used = {}
        def load_xm(tc, h):
            par = (tc * 2 + h) % 2
            self.wait("scalar", "ve", xm_used.get(par, 0))
            xt = self.dma("scalar", f"c_x{par}", xrd[par][:, :],
                          self.i_x[tc * 128:(tc + 1) * 128, h * 1024:(h + 1) * 1024])
            mt = self.dma("scalar", f"c_m{par}", mrd[par][:, :],
                          self.d_moe[tc * 128:(tc + 1) * 128, h * 1024:(h + 1) * 1024])
            xm_t[(tc, h)] = (par, xt, mt)
        load_xm(0, 0)
        load_xm(0, 1)
        for tc in range(NT):
            par = tc % 2
            self.wait("sync", "te", used.get(par, 0))
            cp_t[tc] = self.dma("sync", f"s_cp{par}",
                                cp[par][:, :].rearrange("p (hc f) -> p hc f", hc=ND),
                                self.d_ctx[:, tc * 128:(tc + 1) * 128]
                                .rearrange("(hc p) f -> p hc f", p=128))
            self.wait("tensor", f"s_cp{par}", cp_t[tc])
            for ot in range(4):
                i = tc * 4 + ot
                slot = i % 2
                h = ot // 2
                po = self.bank("B", slot)
                self.wait("tensor", "ve", pe_free.get(slot, 0))
                def mm_o(en, par=par, ot=ot, po=po):
                    for hc in range(ND):
                        en.matmul(po[:, :], cp[par][:, hc * 128:(hc + 1) * 128],
                                  self.big[:, hc * D + ot * 512: hc * D + (ot + 1) * 512],
                                  start=(hc == 0), stop=False)
                    return en.matmul(po[:, :], self.ones_s[0:1, 0:128],
                                     boh[0:1, ot * 512:(ot + 1) * 512], start=False, stop=True)
                ttk = self.emit_tm(mm_o)
                used[par] = ttk
                xpar, xt, mt = xm_t[(tc, h)]
                self.wait("vector", "te", ttk)
                self.wait("vector", f"c_x{xpar}", xt)
                self.wait("vector", f"c_m{xpar}", mt)
                self.wait("vector", f"s_ob{par}", ob_dma.get(par, 0))
                off = (ot % 2) * 512
                self.emit("vector", lambda en, xpar=xpar, ot=ot, po=po, par=par, off=off: (
                    en.tensor_add(tmp[:, :], po[:, :], xrd[xpar][:, off:off + 512]),
                    en.drain(),
                    en.tensor_add(ob[par][:, ot * 512:(ot + 1) * 512], tmp[:, :],
                                  mrd[xpar][:, off:off + 512])))
                vtk = self.mark("vector", "ve")
                pe_free[slot] = vtk
                if ot == 1:
                    xm_used[xpar] = vtk
                    if tc + 1 < NT:
                        load_xm(tc + 1, 0)
                elif ot == 3:
                    xm_used[xpar] = vtk
                    if tc + 1 < NT:
                        load_xm(tc + 1, 1)
            self.wait("scalar", "ve", self.cnt["ve"])
            ob_dma[par] = self.dma("scalar", f"s_ob{par}",
                                   self.o_out[tc * 128:(tc + 1) * 128, :], ob[par][:, :])

    def build(self):
        nc = self.nc
        with ExitStack() as ctx:
            self.ctx = ctx
            self.declare()
            self.alloc_persist(ctx)
            with ExitStack() as chm:
                self.ht = chm.enter_context(nc.sbuf_tensor("ht", [128, ND * self.T], BF16))
                with ExitStack() as cwb:
                    self.wdT = cwb.enter_context(nc.sbuf_tensor("wdT", [8, self.T], F32))
                    self.wb = cwb.enter_context(nc.sbuf_tensor("wb", [128, self.T], F32))
                    self.wb2 = cwb.enter_context(nc.sbuf_tensor("wb2", [128, self.T], F32))
                    with ExitStack() as c0:
                        self.p0_ln(c0)
                    self.barrier()
                    self.p1_ht()
                    with ExitStack() as c2:
                        self.p2_router(c2)
                        with ExitStack() as c3:
                            self.p3_moe(c3)
                            self.stage_weights(c3)
                            self.wait("sync", "te", self.moe_last_h)
                            for dc in range(ND):
                                self.dmat("xn_sync",
                                          self.ht[:, dc * self.T:(dc + 1) * self.T],
                                          self.d_xn[:, dc * 128:(dc + 1) * 128])
                            self.xn_t = self.cnt["xn_sync"]
                self.barrier()
                self.big = ctx.enter_context(
                    nc.sbuf_tensor("big", [128, ND * max(self.T, D)], BF16, side="right"))
                with ExitStack() as c4:
                    self.p4_qkv(c4)
                self.barrier()
            with ExitStack() as c56:
                self.vs_s = c56.enter_context(nc.sbuf_tensor("vs_s", [128, self.NT * D], BF16))
                self.pT = [c56.enter_context(nc.sbuf_tensor(f"pT{i}", [128, self.NT * 512], BF16))
                           for i in range(2)]
                with ExitStack() as c5:
                    self.p5_scores(c5)
                self.barrier()
                with ExitStack() as c6:
                    self.p6_ctx(c6)
                self.barrier()
            with ExitStack() as c7:
                self.p7_out(c7)
            self.barrier()
            block = ctx.enter_context(nc.Block())
            for e in ENGINES:
                def body(eng, e=e):
                    for fn in self.ops[e]:
                        fn(eng)
                getattr(block, e)(body)
        return nc


def build_nc(NT=16, dbg=False):
    nc = bass.Bass("TRN2", target_bir_lowering=False)
    b = Build(nc, NT=NT, dbg=dbg)
    b.build()
    return nc


def make_in_maps(inputs, NT=16):
    T = NT * 128
    x = np.ascontiguousarray(np.asarray(inputs["L0_attn_hidden_states"], dtype=np.float32))
    common = {
        "w_qkv": np.ascontiguousarray(np.asarray(inputs["L0_attn_w_qkv"], dtype=np.float32)),
        "b_qkv": np.ascontiguousarray(np.asarray(inputs["L0_attn_b_qkv"], dtype=np.float32).reshape(1, FQKV)),
        "w_out": np.ascontiguousarray(np.asarray(inputs["L0_attn_w_out"], dtype=np.float32)),
        "b_out": np.ascontiguousarray(np.asarray(inputs["L0_attn_b_out"], dtype=np.float32).reshape(1, D)),
        "rms_w": np.ascontiguousarray(np.asarray(inputs["L0_ffn_rms_weight"], dtype=np.float32).reshape(ND, 128)),
        "router_w": np.ascontiguousarray(np.asarray(inputs["L0_ffn_router_w"], dtype=np.float32)),
        "w_gate": np.ascontiguousarray(np.asarray(inputs["L0_ffn_w_gate"], dtype=np.float32)),
        "w_up": np.ascontiguousarray(np.asarray(inputs["L0_ffn_w_up"], dtype=np.float32)),
        "w_down": np.ascontiguousarray(np.asarray(inputs["L0_ffn_w_down"], dtype=np.float32)),
        "sw_gate": np.ascontiguousarray(np.asarray(inputs["L0_ffn_sw_gate"], dtype=np.float32)),
        "sw_up": np.ascontiguousarray(np.asarray(inputs["L0_ffn_sw_up"], dtype=np.float32)),
        "sw_down": np.ascontiguousarray(np.asarray(inputs["L0_ffn_sw_down"], dtype=np.float32)),
    }
    return [dict(common, x=np.ascontiguousarray(x[c][:T])) for c in range(x.shape[0])]


def kernel(**inputs):
    nc = build_nc(NT=16)
    in_maps = make_in_maps(inputs, NT=16)
    res = run_bass_kernel_spmd(nc, in_maps, core_ids=list(range(8)))
    return np.stack([res.results[c]["out"] for c in range(8)], axis=0)
